# version 10
# baseline (speedup 1.0000x reference)
import contextlib
import numpy as np
import ml_dtypes
import concourse.bass as bass
import concourse.mybir as mybir
from concourse.bass_utils import run_bass_kernel_spmd

F32 = mybir.dt.float32
BF16 = mybir.dt.bfloat16
AF = mybir.ActivationFunctionType
ALU = mybir.AluOpType
AX = mybir.AxisListType

T = 2048
DM = 1024
NH = 16
DH = 64
NHP = 8
NCH = 8
NTT = 16
E_R, E_T, TOPK = 32, 40, 6
SCALE = 2.5
EPS = 1e-5
NEG = -1e30
BIGI = 65536.0
BIG2 = 1e30
WIN, DIL = 128, 4

ENGINES = ("tensor", "vector", "scalar", "sync", "gpsimd")
ESEM = {"tensor": "pe", "vector": "dv", "scalar": "ac", "sync": "sy", "gpsimd": "gp"}

DEBUG = 0
STOP_AFTER = None

C_QW = 0
C_QF = 3072
C_KF = 5120
C_VT = 7168
C_PB = 9216
C_PTS = 11264
C_VS = 11776
C_MSK = 13312
C_XT0, C_XT1, C_LNT = 9216, 10240, 11264
C_WOW = 0
C_XL0, C_XL1 = 8192, 12288
C_LNT2 = 0
C_SC = 0
C_SEL = 640
C_WFL = 1280
C_EQ = 1920
C_MI = 2560
C_T40 = 3200
C_WCAT = 3264
A_WGU = 0
A_WD = 8192
A_SU = (12288, 13312)
A_SUS = 14336
A_OST = (0, 1024)
CV_STG = 14336

N_MASK_TILES = sum(min(3, q + 1) for q in range(NTT)) + 1
T4_COL = (N_MASK_TILES - 1) * 128


def _boundary_index(qi, jb):
    cum = sum(min(3, q + 1) for q in range(qi))
    return cum + (jb - max(0, qi - 2))


def build_mask_host():
    out = np.zeros((128, N_MASK_TILES * 128), dtype=np.float32)
    col = 0
    for qi in range(NTT):
        i = qi * 128 + np.arange(128)[:, None]
        for jb in range(max(0, qi - 2), qi + 1):
            j = jb * 128 + np.arange(128)[None, :]
            win = (j <= i) & (j >= i - WIN + 1)
            base = i - WIN - DIL
            dil = (j <= base) & ((base - j) % DIL == 0)
            out[:, col:col + 128] = np.where(win | dil, 0.0, NEG)
            col += 128
    p = np.arange(128)[:, None]
    jj = np.arange(128)[None, :]
    out[:, col:col + 128] = np.where((p - jj) % DIL == 0, 0.0, NEG)
    return out.astype(ml_dtypes.bfloat16)


class Ctx:

    def __init__(self, blk, sems, eng_name, eng):
        self.blk = blk
        self.sems = sems
        self.counts = {k: 0 for k in sems}
        self.eng_name = eng_name
        self.eng = eng
        self.pending_small = {}
        self.force_small = False

    def at(self, s):
        return self.counts[s]

    def op(self, engine, emit, waits=(), dma=False, sem=None, inc=True, small=False):
        if self.force_small and engine == "vector":
            small = True
        semname = (sem or ESEM[engine]) if inc else None
        amt = 16 if dma else 1
        tgt = None
        if semname is not None:
            self.counts[semname] += amt
            tgt = self.counts[semname]
        psmall = self.pending_small.get(engine)
        self.pending_small[engine] = (semname, tgt) if (small and semname) else None
        if self.eng_name == engine:
            if psmall is not None:
                self.eng.wait_ge(self.sems[psmall[0]], psmall[1])
            for s, v in waits:
                if v > 0:
                    self.eng.wait_ge(self.sems[s], v)
            instr = emit(self.eng)
            if semname is not None:
                instr.then_inc(self.sems[semname], amt)
        return tgt

    def barrier(self):
        snap = dict(self.counts)
        own = ESEM[self.eng_name]
        for s, v in snap.items():
            if v > 0 and s != own:
                self.eng.wait_ge(self.sems[s], v)

    def final_wait(self, engine, semname, value):
        if self.eng_name == engine:
            self.eng.wait_ge(self.sems[semname], value)


def build_program():
    nc = bass.Bass("TRN2", target_bir_lowering=False)
    dr = {}
    dr["x"] = nc.dram_tensor("x", [T, DM], F32, kind="ExternalInput").ap()
    dr["qkv_st"] = nc.dram_tensor("qkv_st", [NHP, 128, 3072], F32, kind="ExternalInput").ap()
    dr["wo_st"] = nc.dram_tensor("wo_st", [128, 8192], F32, kind="ExternalInput").ap()
    dr["wc_st"] = nc.dram_tensor("wc_st", [128, 320], F32, kind="ExternalInput").ap()
    dr["bias_st"] = nc.dram_tensor("bias_st", [128, E_T], F32, kind="ExternalInput").ap()
    dr["gu_st"] = nc.dram_tensor("gu_st", [E_R, 128, 8192], F32, kind="ExternalInput").ap()
    dr["dn_st"] = nc.dram_tensor("dn_st", [E_R, 128, 4096], F32, kind="ExternalInput").ap()
    dr["mask_st"] = nc.dram_tensor("mask_st", [128, N_MASK_TILES * 128], BF16,
                                   kind="ExternalInput").ap()
    dr["out"] = nc.dram_tensor("out", [T, DM], F32, kind="ExternalOutput").ap()
    dr["gu_hl"] = nc.dram_tensor("gu_hl", [E_R, 128, 16384], BF16, kind="Internal").ap()
    dr["dn_hl"] = nc.dram_tensor("dn_hl", [E_R, 128, 8192], BF16, kind="Internal").ap()
    if DEBUG:
        for nm, shape in [("dbg_a", [128, 16384]), ("dbg_b", [128, 16384]),
                          ("dbg_c", [128, 16384]), ("dbg_wfm", [41, 2048]),
                          ("dbg_s", [128, 64])]:
            dr[nm] = nc.dram_tensor(nm, shape, F32, kind="ExternalOutput").ap()

    with contextlib.ExitStack() as est:
        sb = {}
        for nm, shape in [("A", [128, 16384]), ("B", [128, 16384]), ("C", [128, 16384]),
                          ("ident", [128, 128]), ("oh", [41, 128]), ("ohb", [41, 128]),
                          ("wcw", [128, 320]), ("bias40", [128, E_T]), ("iotaB", [128, E_T]),
                          ("wfm", [41, 2048]), ("stg", [128, 512]), ("stats", [128, 12]),
                          ("mv", [128, 2]), ("sig", [128, 1]), ("rinv", [128, 1]),
                          ("negmax", [128, 2]), ("ell", [128, 2]), ("linv", [128, 2]),
                          ("gneg", [128, NTT]), ("gsinv", [128, NTT]), ("zwt", [128, NTT]),
                          ("rmx", [128, NTT]), ("rmn", [128, NTT]),
                          ("wbs", [128, 512])] + ([("dbgs", [128, 64])] if DEBUG else []):
            sb[nm] = est.enter_context(nc.sbuf_tensor(nm, shape, F32))
        ps = [est.enter_context(nc.psum_tensor(f"ps{i}", [128, 512], F32)) for i in range(8)]
        semnames = ["pe", "dv", "ac", "sy", "gp",
                    "xs0", "xs1", "cq", "cst", "gu", "dn", "st0", "st1"]
        sems = {s: est.enter_context(nc.semaphore(s)) for s in semnames}
        blk = est.enter_context(nc.Block())

        for en in ENGINES:
            def section(eng, en=en):
                ctx = Ctx(blk, sems, en, eng)
                ctx.nc = nc
                emit_all(ctx, dr, sb, ps)
            getattr(blk, en)(section)
    return nc


def emit_all(ctx, dr, sb, ps):
    op = ctx.op
    A, B, C = sb["A"], sb["B"], sb["C"]
    ident = sb["ident"]
    masksv = C[:, C_MSK:C_MSK + N_MASK_TILES * 64].bitcast(BF16)
    A8 = A[:].rearrange("p (c t) -> p c t", c=8)
    B8 = B[:].rearrange("p (c t) -> p c t", c=8)
    C8 = C[:].rearrange("p (c t) -> p c t", c=8)

    v = op("gpsimd", lambda e: e.iota(ident[:], pattern=[[-1, 128]], base=0,
                                      channel_multiplier=1, allow_small_or_imprecise_dtypes=True))
    op("vector", lambda e: e.tensor_scalar(ident[:], ident[:], 0.0, None, ALU.is_equal),
       waits=[("gp", v)])
    v_ident = ctx.at("dv")
    v = op("gpsimd", lambda e: e.iota(sb["iotaB"][:], pattern=[[1, E_T]], base=0,
                                      channel_multiplier=0, allow_small_or_imprecise_dtypes=True))
    op("vector", lambda e: e.tensor_scalar_add(sb["iotaB"][:], sb["iotaB"][:], BIGI),
       waits=[("gp", v)])
    op("gpsimd", lambda e: e.iota(sb["ohb"][:], pattern=[[0, 128]], base=0,
                                  channel_multiplier=1, allow_small_or_imprecise_dtypes=True))
    v_ohb = ctx.at("gp")
    op("sync", lambda e: e.dma_start(out=sb["wcw"][:], in_=dr["wc_st"]), dma=True, sem="cst")
    op("sync", lambda e: e.dma_start(out=sb["bias40"][:], in_=dr["bias_st"]), dma=True, sem="cst")
    v_mask_ld = op("sync", lambda e: e.dma_start(out=masksv, in_=dr["mask_st"]), dma=True, sem="cst")
    v_qw_ld = op("sync", lambda e: e.dma_start(out=C[:, C_QW:C_QW + 3072],
                                               in_=dr["qkv_st"][0]), dma=True, sem="cq")

    def stopped(phase):
        if STOP_AFTER != phase:
            return False
        if DEBUG:
            ctx.barrier()
            for nm, src in [("dbg_a", sb["A"][:]), ("dbg_b", sb["B"][:]),
                            ("dbg_c", sb["C"][:]), ("dbg_wfm", sb["wfm"][:]),
                            ("dbg_s", sb["dbgs"][:])]:
                op("gpsimd", lambda e, nm=nm, src=src: e.dma_start(out=dr[nm], in_=src),
                   dma=True, sem="st0")
            ctx.final_wait("gpsimd", "st0", ctx.at("st0"))
        return True

    cvst = {"out1": 0}

    def convert_experts(es):
        C7 = sb["B"]
        stg_in = C7[:, CV_STG:CV_STG + 1024]
        stg_hi = C7[:, CV_STG + 1024:CV_STG + 1536].bitcast(BF16)
        stg_lo = C7[:, CV_STG + 1536:CV_STG + 2048].bitcast(BF16)
        for e in es:
            for q in range(12):
                if q < 8:
                    src = dr["gu_st"][e][:, q * 1024:(q + 1) * 1024]
                    dhi = dr["gu_hl"][e][:, q * 1024:(q + 1) * 1024]
                    dlo = dr["gu_hl"][e][:, 8192 + q * 1024:8192 + (q + 1) * 1024]
                else:
                    qq = q - 8
                    src = dr["dn_st"][e][:, qq * 1024:(qq + 1) * 1024]
                    dhi = dr["dn_hl"][e][:, qq * 1024:(qq + 1) * 1024]
                    dlo = dr["dn_hl"][e][:, 4096 + qq * 1024:4096 + (qq + 1) * 1024]
                v_in = op("sync", lambda ee, src=src: ee.dma_start(out=stg_in, in_=src),
                          waits=[("dn", cvst["out1"])], dma=True, sem="gu")
                op("vector", lambda ee: ee.tensor_copy(stg_hi, stg_in),
                   waits=[("gu", v_in), ("dn", cvst["out1"])])
                v_sp = op("vector", lambda ee: ee.tensor_tensor(
                    stg_lo, stg_in, stg_hi, ALU.subtract))
                op("sync", lambda ee, dhi=dhi: ee.dma_start(
                    out=dhi, in_=stg_hi), waits=[("dv", v_sp)], dma=True, sem="dn")
                cvst["out1"] = op("sync", lambda ee, dlo=dlo: ee.dma_start(
                    out=dlo, in_=stg_lo), dma=True, sem="dn")

    with ctx.nc.named_scope("p1_ln1"):
        ln_to_fm(ctx, sb, ps, dst8=A8, src_kind="hbm", dr=dr, v_ident=v_ident)
    convert_experts(range(0, 2))
    ctx.barrier()
    if stopped("ln1"):
        return

    st = {"sbank_free": [None] * 4,
          "p4_free": 0,
          "obank_free": [0, 0],
          "pts_free": 0,
          "p_free": [0, 0],
          "exp_done": [0, 0],
          "qkv_bank_free": [None] * 4,
          "vs_free": 0, "vt_free": 0,
          "qw_ld": v_qw_ld}
    with ctx.nc.named_scope("p2_attn"):
        for hp in range(NHP):
            st["conv_dn"] = cvst["out1"]
            attn_headpair(ctx, sb, ps, dr, hp, st, v_ident, v_mask_ld)
            if hp < 6:
                convert_experts(range(2 + hp * 5, 2 + (hp + 1) * 5))
    ctx.barrier()
    if stopped("attn"):
        return

    v_wo_ld = op("sync", lambda e: e.dma_start(out=C[:, C_WOW:C_WOW + 8192],
                                               in_=dr["wo_st"]), dma=True, sem="cst")
    with ctx.nc.named_scope("p3_wo"):
        wo_phase(ctx, sb, ps, dr, v_wo_ld, v_ident)
    ctx.barrier()
    if stopped("wo"):
        return

    with ctx.nc.named_scope("p4_ln2"):
        ln_to_fm(ctx, sb, ps, dst8=B8, src_kind="A", dr=dr, v_ident=v_ident)
    ctx.barrier()
    if stopped("ln2"):
        return

    with ctx.nc.named_scope("p5_route"):
        routing(ctx, sb, ps, v_ident)
    ctx.barrier()
    if stopped("route"):
        return

    with ctx.nc.named_scope("p6_init"):
        v_accinit_tp = acc_init(ctx, sb, ps, v_ident, v_ohb)
    Bhi_full = B[:, 0:8192].bitcast(BF16)
    Blo_full = B[:, 8192:16384].bitcast(BF16)
    op("vector", lambda e: e.tensor_copy(A[:], B[:]), waits=[("pe", v_accinit_tp)])
    op("vector", lambda e: e.tensor_copy(Bhi_full, A[:]))
    v_conv = op("vector", lambda e: e.tensor_tensor(Blo_full, A[:], Bhi_full, ALU.subtract))
    v_gu_ld = op("sync", lambda e: e.dma_start(
        out=A[:, 0:8192].bitcast(BF16), in_=dr["gu_hl"][0]),
        waits=[("pe", v_accinit_tp), ("dv", v_conv)], dma=True, sem="gu")
    v_dn_ld = op("sync", lambda e: e.dma_start(
        out=A[:, 8192:12288].bitcast(BF16), in_=dr["dn_hl"][0]),
        waits=[("pe", v_accinit_tp)], dma=True, sem="dn")
    ctx.barrier()
    if stopped("init"):
        return

    with ctx.nc.named_scope("p7_moe"):
        moe_experts(ctx, sb, ps, dr, v_gu_ld, v_dn_ld)
    ctx.barrier()
    if stopped("moe"):
        return

    if DEBUG:
        vs = []
        vs.append(op("gpsimd", lambda e: e.dma_start(out=dr["dbg_a"], in_=A[:]), dma=True, sem="st0"))
        vs.append(op("gpsimd", lambda e: e.dma_start(out=dr["dbg_b"], in_=B[:]), dma=True, sem="st0"))
        vs.append(op("gpsimd", lambda e: e.dma_start(out=dr["dbg_c"], in_=C[:]), dma=True, sem="st0"))
        vs.append(op("gpsimd", lambda e: e.dma_start(out=dr["dbg_wfm"], in_=sb["wfm"][:]),
                     dma=True, sem="st0"))

    n_tiles = int(STOP_AFTER[3:]) if (STOP_AFTER or "").startswith("out") else NTT
    with ctx.nc.named_scope("p8_out"):
        out_phase(ctx, sb, ps, dr, v_ident, n_tiles)
    ctx.final_wait("gpsimd", "st0", ctx.at("st0"))
    ctx.final_wait("gpsimd", "st1", ctx.at("st1"))


def ln_to_fm(ctx, sb, ps, dst8, src_kind, dr, v_ident):
    op = ctx.op
    C = sb["C"]
    A = sb["A"]
    last_lnt_reader = 0
    last_xt_reader = [0, 0]
    bank_free = [0, 0]
    lnt_off = C_LNT if src_kind == "hbm" else C_LNT2
    for tt in range(NTT):
        buf = tt % 2
        if src_kind == "hbm":
            xt_off = C_XT0 if buf == 0 else C_XT1
            xsem = "xs0" if buf == 0 else "xs1"
            v_x = op("gpsimd", lambda e, tt=tt, xt_off=xt_off: e.dma_start(
                out=C[:, xt_off:xt_off + 1024],
                in_=dr["x"][tt * 128:(tt + 1) * 128, :]),
                waits=[("dv", last_xt_reader[buf])], dma=True, sem=xsem)
            src = C[:, xt_off:xt_off + 1024]
            w0 = [(xsem, v_x)]
        else:
            src = A[:, tt * 1024:(tt + 1) * 1024]
            w0 = []
        v_red = op("vector", lambda e, src=src: e.reduce_sum(sb["mv"][:, 0:1], src, axis=AX.X),
                   waits=w0, small=True)
        v_sq = op("scalar", lambda e, src=src, lnt_off=lnt_off: e.activation(
            C[:, lnt_off:lnt_off + 1024], src, AF.Square, bias=0.0, scale=1.0,
            accum_out=sb["mv"][:, 1:2]),
            waits=w0 + [("pe", last_lnt_reader), ("dv", v_red)])
        op("vector", lambda e: e.tensor_scalar_mul(sb["mv"][:, 0:1], sb["mv"][:, 0:1], 1.0 / 1024.0),
           waits=[("ac", v_sq)], small=True)
        op("vector", lambda e: e.tensor_tensor(sb["stats"][:, 0:1], sb["mv"][:, 0:1],
                                               sb["mv"][:, 0:1], ALU.mult), small=True)
        op("vector", lambda e: e.tensor_scalar(sb["sig"][:], sb["mv"][:, 1:2], 1.0 / 1024.0,
                                               sb["stats"][:, 0:1], ALU.mult, ALU.subtract), small=True)
        v_sig = op("vector", lambda e: e.tensor_scalar_add(sb["sig"][:], sb["sig"][:], EPS),
                   small=True)
        v_sqrt = op("scalar", lambda e: e.activation(sb["sig"][:], sb["sig"][:], AF.Sqrt,
                                                     bias=0.0, scale=1.0),
                    waits=[("dv", v_sig)])
        op("vector", lambda e: e.reciprocal(sb["rinv"][:], sb["sig"][:]), waits=[("ac", v_sqrt)],
           small=True)
        if DEBUG and src_kind == "hbm":
            op("vector", lambda e, tt=tt: e.tensor_copy(sb["dbgs"][:, tt*4:tt*4+2], sb["mv"][:]))
            op("vector", lambda e, tt=tt: e.tensor_copy(sb["dbgs"][:, tt*4+2:tt*4+3], sb["sig"][:]))
            op("vector", lambda e, tt=tt: e.tensor_copy(sb["dbgs"][:, tt*4+3:tt*4+4], sb["rinv"][:]))
        v_ln = op("vector", lambda e, src=src, lnt_off=lnt_off: e.tensor_scalar(
            C[:, lnt_off:lnt_off + 1024], src,
            sb["mv"][:, 0:1], sb["rinv"][:], ALU.subtract, ALU.mult),
            waits=[("pe", last_lnt_reader)])
        if src_kind == "hbm":
            last_xt_reader[buf] = v_ln
        for half in range(2):
            bank = half
            for k in range(4):
                cchunk = half * 4 + k
                w = [("dv", v_ln), ("dv", v_ident)]
                if k == 0:
                    w.append(("ac", bank_free[bank]))
                op("tensor", lambda e, cchunk=cchunk, bank=bank, k=k, lnt_off=lnt_off: e.transpose(
                    ps[bank][:, k * 128:(k + 1) * 128],
                    C[:, lnt_off + cchunk * 128:lnt_off + (cchunk + 1) * 128],
                    sb["ident"][:]),
                    waits=w, inc=(k == 3))
            v_tp = ctx.at("pe")
            last_lnt_reader = v_tp
            op("scalar", lambda e, half=half, tt=tt, bank=bank: e.copy(
                dst8[:, half * 4:half * 4 + 4, tt * 128:(tt + 1) * 128],
                ps[bank][:, 0:512].rearrange("p (c t) -> p c t", c=4)),
                waits=[("pe", v_tp)])
            bank_free[bank] = ctx.at("ac")


def attn_headpair(ctx, sb, ps, dr, hp, st, v_ident, v_mask_ld):
    op = ctx.op
    A, C = sb["A"], sb["C"]
    A8 = A[:].rearrange("p (c t) -> p c t", c=8)

    bank_rot = 0
    for m in range(3):
        for ts in range(4):
            bank = bank_rot % 4
            bank_rot += 1
            for kc in range(NCH):
                w = [("cq", st["qw_ld"])]
                if kc == 0 and st["qkv_bank_free"][bank] is not None:
                    w.append(st["qkv_bank_free"][bank])
                op("tensor", lambda e, m=m, ts=ts, kc=kc, bank=bank: e.matmul(
                    ps[bank][:, 0:512],
                    C[:, C_QW + (kc * 3 + m) * 128:C_QW + (kc * 3 + m) * 128 + 128],
                    A8[:, kc, ts * 512:(ts + 1) * 512],
                    start=(kc == 0), stop=(kc == NCH - 1)),
                    waits=w, inc=(kc == NCH - 1))
            v_mm = ctx.at("pe")
            if m < 2:
                dst_off = C_QF if m == 0 else C_KF
                op("vector", lambda e, dst_off=dst_off, ts=ts, bank=bank: e.tensor_copy(
                    C[:, dst_off + ts * 512:dst_off + (ts + 1) * 512], ps[bank][:, 0:512]),
                    waits=[("pe", v_mm)])
                st["qkv_bank_free"][bank] = ("dv", ctx.at("dv"))
            else:
                v_vs = op("scalar", lambda e, bank=bank: e.copy(
                    C[:, C_VS:C_VS + 512], ps[bank][:, 0:512]),
                    waits=[("pe", v_mm), ("pe", st["vs_free"])])
                st["qkv_bank_free"][bank] = ("ac", v_vs)
                for k in range(4):
                    w = [("ac", v_vs), ("dv", v_ident)]
                    if k == 0:
                        w.append(("ac", st["p4_free"]))
                        w.append(("dv", st["vt_free"]))
                    op("tensor", lambda e, k=k: e.transpose(
                        ps[4][:, k * 128:(k + 1) * 128],
                        C[:, C_VS + k * 128:C_VS + (k + 1) * 128], sb["ident"][:]),
                        waits=w, inc=(k == 3))
                v_tp = ctx.at("pe")
                st["vs_free"] = v_tp
                op("vector", lambda e, ts=ts: e.tensor_copy(
                    C[:, C_VT + ts * 512:C_VT + (ts + 1) * 512], ps[4][:, 0:512]),
                    waits=[("pe", v_tp)])
                st["vt_free"] = ctx.at("dv")
    v_qkv_done = ctx.at("pe")
    for b in range(4):
        st["sbank_free"][b] = st["qkv_bank_free"][b]
    if hp + 1 < NHP:
        st["qw_ld"] = op("sync", lambda e, hp=hp: e.dma_start(
            out=C[:, C_QW:C_QW + 3072], in_=dr["qkv_st"][hp + 1]),
            waits=[("pe", v_qkv_done)], dma=True, sem="cq")

    def p_ap(par, lo, hi):
        if par == 0:
            return C[:, C_PB + lo:C_PB + hi]
        if hp == NHP - 1:
            return C[:, C_QW + lo:C_QW + hi]
        return sb["B"][:, (hp + 1) * 2048 + lo:(hp + 1) * 2048 + hi]

    def emit_ptpv(pend):
        head, qi, par, v_norm = pend
        h64 = head * 64
        jn = (qi + 1) * 128
        njc = (jn + 511) // 512
        first = True
        for jc in range(njc):
            w = min(512, jn - jc * 512)
            nk = w // 128
            for k in range(nk):
                wts = [("dv", v_norm), ("dv", v_ident)]
                if k == 0:
                    wts.append(("ac", st["p4_free"]))
                op("tensor", lambda e, jc=jc, k=k, par=par: e.transpose(
                    ps[4][:, k * 128:(k + 1) * 128],
                    p_ap(par, jc * 512 + k * 128, jc * 512 + (k + 1) * 128),
                    sb["ident"][:]),
                    waits=wts, inc=(k == nk - 1))
            v_pt = ctx.at("pe")
            st["p_free"][par] = v_pt
            v_ptc = op("scalar", lambda e, w=w: e.copy(
                C[:, C_PTS:C_PTS + w], ps[4][:, 0:w]),
                waits=[("pe", v_pt), ("pe", st["pts_free"])])
            st["p4_free"] = v_ptc
            for k in range(nk):
                wts = [("ac", v_ptc)]
                if first:
                    wts.append(("ac", st["obank_free"][head]))
                last = (jc == njc - 1 and k == nk - 1)
                op("tensor", lambda e, jc=jc, k=k, h64=h64, head=head, first=first,
                   last=last: e.matmul(
                    ps[5 + head][h64:h64 + 64, 0:128],
                    C[:, C_VT + (jc * 4 + k) * 128 + h64:C_VT + (jc * 4 + k) * 128 + h64 + 64],
                    C[:, C_PTS + k * 128:C_PTS + (k + 1) * 128],
                    start=first, stop=last,
                    tile_position=(0, h64)),
                    waits=wts, inc=last)
                first = False
        v_pv = ctx.at("pe")
        st["pts_free"] = v_pv
        op("scalar", lambda e, h64=h64, qi=qi, head=head: e.copy(
            sb["B"][h64:h64 + 64, hp * 2048 + qi * 128:hp * 2048 + (qi + 1) * 128],
            ps[5 + head][h64:h64 + 64, 0:128]),
            waits=[("pe", v_pv)])
        st["obank_free"][head] = ctx.at("ac")

    pend = None
    for head in range(2):
        h64 = head * 64
        for qi in range(NTT):
            it = head * NTT + qi
            par = it % 2
            jn = (qi + 1) * 128
            njc = (jn + 511) // 512
            s_done = []
            for jc in range(njc):
                w = min(512, jn - jc * 512)
                wts = [("dv", st["vt_free"])]
                if st["sbank_free"][jc] is not None:
                    wts.append(st["sbank_free"][jc])
                op("tensor", lambda e, jc=jc, w=w, qi=qi, h64=h64: e.matmul(
                    ps[jc][:, 0:w],
                    C[h64:h64 + 64, C_QF + qi * 128:C_QF + (qi + 1) * 128],
                    C[h64:h64 + 64, C_KF + jc * 512:C_KF + jc * 512 + w],
                    start=True, stop=True),
                    waits=wts)
                s_done.append(ctx.at("pe"))
            for jc in range(njc):
                w = min(512, jn - jc * 512)
                nblocks = w // 128
                jb0 = jc * 4
                npure = min(max(qi - 2 - jb0, 0), nblocks)
                waits = [("pe", s_done[jc]), ("pe", st["p_free"][par]),
                         ("cst", v_mask_ld)]
                if hp == NHP - 2 and par == 1:
                    waits.append(("dn", st.get("conv_dn", 0)))
                if npure > 0:
                    op("vector", lambda e, jc=jc, npure=npure, par=par: e.scalar_tensor_tensor(
                        p_ap(par, jc * 512, jc * 512 + npure * 128),
                        ps[jc][:, 0:npure * 128], 0.125,
                        masksv_t4(sb).unsqueeze(1).broadcast_to([128, npure, 128]),
                        ALU.mult, ALU.add),
                        waits=waits)
                    waits = []
                if npure < nblocks:
                    nb = nblocks - npure
                    i0 = _boundary_index(qi, jb0 + npure)
                    op("vector", lambda e, jc=jc, npure=npure, nb=nb, i0=i0, par=par:
                       e.scalar_tensor_tensor(
                        p_ap(par, jc * 512 + npure * 128, jc * 512 + (npure + nb) * 128),
                        ps[jc][:, npure * 128:(npure + nb) * 128], 0.125,
                        masksv_all(sb)[:, i0 * 128:(i0 + nb) * 128],
                        ALU.mult, ALU.add),
                        waits=waits)
                st["sbank_free"][jc] = ("dv", ctx.at("dv"))
            v_max = op("vector", lambda e, jn=jn, par=par: e.reduce_max(
                sb["negmax"][:, par:par + 1], p_ap(par, 0, jn), axis=AX.X, negate=True),
                waits=[("ac", st["exp_done"][par])], small=True)
            v_exp = op("scalar", lambda e, jn=jn, par=par: e.activation(
                p_ap(par, 0, jn), p_ap(par, 0, jn), AF.Exp,
                bias=sb["negmax"][:, par:par + 1], scale=1.0,
                accum_out=sb["ell"][:, par:par + 1]),
                waits=[("dv", v_max)])
            st["exp_done"][par] = v_exp
            op("vector", lambda e, par=par: e.reciprocal(
                sb["linv"][:, par:par + 1], sb["ell"][:, par:par + 1]),
               waits=[("ac", v_exp)], small=True)
            v_norm = op("vector", lambda e, jn=jn, par=par: e.tensor_scalar_mul(
                p_ap(par, 0, jn), p_ap(par, 0, jn), sb["linv"][:, par:par + 1]))
            if pend is not None:
                emit_ptpv(pend)
            pend = (head, qi, par, v_norm)
    emit_ptpv(pend)


def masksv_t4(sb):
    return sb["C"][:, C_MSK:C_MSK + N_MASK_TILES * 64].bitcast(BF16)[:, T4_COL:T4_COL + 128]


def masksv_all(sb):
    return sb["C"][:, C_MSK:C_MSK + N_MASK_TILES * 64].bitcast(BF16)


def wo_phase(ctx, sb, ps, dr, v_wo_ld, v_ident):
    op = ctx.op
    A, B, C = sb["A"], sb["B"], sb["C"]
    B8 = B[:].rearrange("p (c t) -> p c t", c=8)
    xl_reader = [0, 0]
    v_xl = [0, 0]
    mmbank_free = [0, 0]
    tpbank_free = [0, 0]
    stg_free = 0
    for ts in range(4):
        buf = ts % 2
        xoff = C_XL0 if buf == 0 else C_XL1
        xsem = "xs0" if buf == 0 else "xs1"
        v_xl[buf] = op("gpsimd", lambda e, ts=ts, xoff=xoff: e.dma_start(
            out=C[:, xoff:xoff + 4096].rearrange("p (a f) -> p a f", a=4),
            in_=dr["x"][ts * 512:(ts + 1) * 512, :].rearrange("(a p) f -> p a f", p=128)),
            waits=[("dv", xl_reader[buf])], dma=True, sem=xsem)
        for oc in range(NCH):
            bank = oc % 2
            for kc in range(NCH):
                w = [("cst", v_wo_ld)]
                if kc == 0:
                    w.append(("ac", mmbank_free[bank]))
                op("tensor", lambda e, oc=oc, kc=kc, ts=ts, bank=bank: e.matmul(
                    ps[bank][:, 0:512],
                    C[:, C_WOW + kc * 1024 + oc * 128:C_WOW + kc * 1024 + (oc + 1) * 128],
                    B8[:, kc, ts * 512:(ts + 1) * 512],
                    start=(kc == 0), stop=(kc == NCH - 1)),
                    waits=w, inc=(kc == NCH - 1))
            v_mm = ctx.at("pe")
            v_stg = op("scalar", lambda e, bank=bank: e.copy(
                sb["stg"][:], ps[bank][:, 0:512]),
                waits=[("pe", v_mm), ("pe", stg_free)])
            mmbank_free[bank] = v_stg
            tb = 2 + (oc % 2)
            for k in range(4):
                w = [("ac", v_stg), ("dv", v_ident)]
                if k == 0:
                    w.append(("dv", tpbank_free[tb - 2]))
                op("tensor", lambda e, k=k, tb=tb: e.transpose(
                    ps[tb][:, k * 128:(k + 1) * 128],
                    sb["stg"][:, k * 128:(k + 1) * 128], sb["ident"][:]),
                    waits=w, inc=(k == 3))
            v_tp = ctx.at("pe")
            stg_free = v_tp
            op("vector", lambda e, ts=ts, oc=oc, tb=tb, xoff=xoff: e.tensor_tensor(
                A[:].rearrange("p (a f) -> p a f", a=NTT)[:, ts * 4:ts * 4 + 4, oc * 128:(oc + 1) * 128],
                ps[tb][:, 0:512].rearrange("p (a f) -> p a f", a=4),
                C[:, xoff:xoff + 4096].rearrange("p (a f) -> p a f", a=4)[:, :, oc * 128:(oc + 1) * 128],
                ALU.add),
                waits=[("pe", v_tp), ("xs0" if buf == 0 else "xs1", v_xl[buf])])
            tpbank_free[tb - 2] = ctx.at("dv")
            xl_reader[buf] = ctx.at("dv")


def routing(ctx, sb, ps, v_ident):
    op = ctx.op
    ctx.force_small = True
    B, C = sb["B"], sb["C"]
    B8 = B[:].rearrange("p (c t) -> p c t", c=8)
    sc = C[:, C_SC:C_SC + 640]
    sel = C[:, C_SEL:C_SEL + 640]
    wfl = C[:, C_WFL:C_WFL + 640]
    eq = C[:, C_EQ:C_EQ + 640]
    mi = C[:, C_MI:C_MI + 640]
    t40 = C[:, C_T40:C_T40 + 40]
    wcat = C[:, C_WCAT:C_WCAT + 656]

    bank_free = [0, 0]
    for tt in range(NTT):
        bank = tt % 2
        for kc in range(NCH):
            w = []
            if kc == 0:
                w.append(("dv", bank_free[bank]))
            op("tensor", lambda e, tt=tt, kc=kc, bank=bank: e.matmul(
                ps[bank][:, 0:E_T],
                B8[:, kc, tt * 128:(tt + 1) * 128],
                sb["wcw"][:, kc * E_T:(kc + 1) * E_T],
                start=(kc == 0), stop=(kc == NCH - 1)),
                waits=w, inc=(kc == NCH - 1))
        v_mm = ctx.at("pe")
        op("vector", lambda e, tt=tt, bank=bank: e.tensor_copy(
            sc[:, tt * 40:(tt + 1) * 40], ps[bank][:, 0:E_T]),
            waits=[("pe", v_mm)])
        bank_free[bank] = ctx.at("dv")
    sc3 = sc.rearrange("p (g x) -> p g x", g=NTT)
    op("vector", lambda e: e.reduce_max(sb["gneg"][:], sc3, axis=AX.X, negate=True))
    v_gm = ctx.at("dv")
    for tt in range(NTT):
        op("scalar", lambda e, tt=tt: e.activation(
            sc[:, tt * 40:(tt + 1) * 40], sc[:, tt * 40:(tt + 1) * 40], AF.Exp,
            bias=sb["gneg"][:, tt:tt + 1], scale=1.0),
            waits=[("dv", v_gm)])
    v_exp = ctx.at("ac")
    op("vector", lambda e: e.reduce_sum(sb["gsinv"][:], sc3, axis=AX.X), waits=[("ac", v_exp)])
    op("vector", lambda e: e.reciprocal(sb["gsinv"][:], sb["gsinv"][:]))
    for tt in range(NTT):
        op("vector", lambda e, tt=tt: e.tensor_scalar_mul(
            sc[:, tt * 40:(tt + 1) * 40], sc[:, tt * 40:(tt + 1) * 40],
            sb["gsinv"][:, tt:tt + 1]))
    for tt in range(NTT):
        op("vector", lambda e, tt=tt: e.tensor_add(
            sel[:, tt * 40:(tt + 1) * 40], sc[:, tt * 40:(tt + 1) * 40], sb["bias40"][:]))
    op("vector", lambda e: e.memset(wfl, 0.0))
    for _k in range(TOPK):
        op("vector", lambda e: e.reduce_max(sb["rmx"][:],
                                            sel.rearrange("p (g x) -> p g x", g=NTT), axis=AX.X))
        for tt in range(NTT):
            op("vector", lambda e, tt=tt: e.tensor_scalar(
                eq[:, tt * 40:(tt + 1) * 40], sel[:, tt * 40:(tt + 1) * 40],
                sb["rmx"][:, tt:tt + 1], None, ALU.is_equal))
        for tt in range(NTT):
            op("vector", lambda e, tt=tt: e.scalar_tensor_tensor(
                mi[:, tt * 40:(tt + 1) * 40], eq[:, tt * 40:(tt + 1) * 40],
                -BIGI, sb["iotaB"][:], ALU.mult, ALU.add))
        op("vector", lambda e: e.tensor_reduce(sb["rmn"][:],
                                               mi.rearrange("p (g x) -> p g x", g=NTT),
                                               axis=AX.X, op=ALU.min))
        for tt in range(NTT):
            op("vector", lambda e, tt=tt: e.tensor_scalar(
                eq[:, tt * 40:(tt + 1) * 40], mi[:, tt * 40:(tt + 1) * 40],
                sb["rmn"][:, tt:tt + 1], None, ALU.is_equal))
        for tt in range(NTT):
            op("vector", lambda e, tt=tt: e.scalar_tensor_tensor(
                t40, sc[:, tt * 40:(tt + 1) * 40], SCALE,
                eq[:, tt * 40:(tt + 1) * 40], ALU.mult, ALU.mult))
            op("vector", lambda e, tt=tt: e.tensor_add(
                wfl[:, tt * 40:(tt + 1) * 40], wfl[:, tt * 40:(tt + 1) * 40], t40))
            op("vector", lambda e, tt=tt: e.scalar_tensor_tensor(
                sel[:, tt * 40:(tt + 1) * 40], eq[:, tt * 40:(tt + 1) * 40],
                -BIG2, sel[:, tt * 40:(tt + 1) * 40], ALU.mult, ALU.add))
    op("vector", lambda e: e.reduce_sum(
        sb["zwt"][:], wfl.rearrange("p (g x) -> p g x", g=NTT)[:, :, E_R:E_T], axis=AX.X))
    op("vector", lambda e: e.tensor_copy(
        wcat.rearrange("p (g x) -> p g x", g=NTT)[:, :, 0:40],
        wfl.rearrange("p (g x) -> p g x", g=NTT)))
    op("vector", lambda e: e.tensor_copy(
        wcat.rearrange("p (g x) -> p g x", g=NTT)[:, :, 40:41],
        sb["zwt"][:].unsqueeze(2)))
    v_wcat = ctx.at("dv")
    bank_free2 = 0
    for tt in range(NTT):
        w = [("dv", v_wcat), ("dv", v_ident)]
        if bank_free2:
            w.append(("ac", bank_free2))
        op("tensor", lambda e, tt=tt: e.transpose(
            ps[2][0:41, 0:128],
            wcat[:, tt * 41:(tt + 1) * 41], sb["ident"][:]), waits=w)
        v_tp = ctx.at("pe")
        op("scalar", lambda e, tt=tt: e.copy(
            sb["wfm"][:, tt * 128:(tt + 1) * 128], ps[2][0:41, 0:128]),
            waits=[("pe", v_tp)])
        bank_free2 = ctx.at("ac")
    ctx.force_small = False


def acc_init(ctx, sb, ps, v_ident, v_ohb):
    op = ctx.op
    A, B, C = sb["A"], sb["B"], sb["C"]
    C8 = C[:].rearrange("p (c t) -> p c t", c=8)
    bank_free = [0, 0]
    for tt in range(NTT):
        for half in range(2):
            bank = half
            for k in range(4):
                cchunk = half * 4 + k
                w = [("dv", v_ident)]
                if k == 0:
                    w.append(("dv", bank_free[bank]))
                op("tensor", lambda e, tt=tt, cchunk=cchunk, bank=bank, k=k: e.transpose(
                    ps[bank][:, k * 128:(k + 1) * 128],
                    A[:, tt * 1024 + cchunk * 128:tt * 1024 + (cchunk + 1) * 128],
                    sb["ident"][:]),
                    waits=w, inc=(k == 3))
            v_tp = ctx.at("pe")
            op("vector", lambda e, half=half, tt=tt, bank=bank: e.tensor_copy(
                C8[:, half * 4:half * 4 + 4, tt * 128:(tt + 1) * 128],
                ps[bank][:, 0:512].rearrange("p (c t) -> p c t", c=4)),
                waits=[("pe", v_tp)])
            bank_free[bank] = ctx.at("dv")
    v_accinit_tp = ctx.at("pe")
    v_accinit_cp = ctx.at("dv")
    op("vector", lambda e: e.tensor_scalar(
        sb["oh"][:], sb["ohb"][:], 40.0, None, ALU.is_equal),
        waits=[("gp", v_ohb)])
    v_oh = ctx.at("dv")
    ps4_free = v_accinit_cp
    for ts in range(4):
        v_mm = op("tensor", lambda e, ts=ts: e.matmul(
            ps[4][:, 0:512], sb["oh"][:],
            sb["wfm"][0:41, ts * 512:(ts + 1) * 512],
            start=True, stop=True),
            waits=[("dv", v_oh), ("dv", ps4_free)])
        for c in range(8):
            op("vector", lambda e, c=c, ts=ts: e.tensor_tensor(
                sb["stg"][:], sb["B"][:, c * 2048 + ts * 512:c * 2048 + (ts + 1) * 512],
                ps[4][:, 0:512], ALU.mult),
                waits=[("pe", v_mm)], small=True)
            op("vector", lambda e, c=c, ts=ts: e.tensor_tensor(
                C[:, c * 2048 + ts * 512:c * 2048 + (ts + 1) * 512],
                C[:, c * 2048 + ts * 512:c * 2048 + (ts + 1) * 512],
                sb["stg"][:], ALU.add), small=True)
        ps4_free = ctx.at("dv")
    return v_accinit_tp


def moe_experts(ctx, sb, ps, dr, v_gu_ld, v_dn_ld):
    op = ctx.op
    A, B, C = sb["A"], sb["B"], sb["C"]
    TS = 256
    NTS = T // TS
    NSLOT = E_R * NTS
    Bhi3 = B[:, 0:8192].bitcast(BF16).rearrange("p (c t) -> p c t", c=8)
    Blo3 = B[:, 8192:16384].bitcast(BF16).rearrange("p (c t) -> p c t", c=8)
    whi = A[:, 0:4096].bitcast(BF16)
    wlo = A[:, 4096:8192].bitcast(BF16)
    dhi = A[:, 8192:10240].bitcast(BF16)
    dlo = A[:, 10240:12288].bitcast(BF16)
    sus = A[:, A_SUS:A_SUS + 1024]
    v_dv0 = ctx.at("dv")
    st = {
        "gu_rdy": v_gu_ld, "dn_rdy": v_dn_ld,
        "gu_bank_free": [("dv", v_dv0)] * 4,
        "dn_bank_free": [v_dv0] * 4,
        "su_free": [0, 0],
        "oh_used": ctx.at("pe"),
        "wbs_ready": [v_dv0, v_dv0],
        "oh_cur": -1,
        "stage_dma": None,
    }

    def build_wb(slot, p4_wait):
        e2, ts2 = slot // NTS, slot % NTS
        if e2 != st["oh_cur"]:
            st["oh_cur"] = e2
            op("vector", lambda ee, e2=e2: ee.tensor_scalar(
                sb["oh"][:], sb["ohb"][:], float(e2), None, ALU.is_equal),
                waits=[("pe", st["oh_used"])], small=True)
        v_oh = ctx.at("dv")
        v_mm = op("tensor", lambda ee, ts2=ts2: ee.matmul(
            ps[4][:, 0:TS], sb["oh"][:],
            sb["wfm"][0:41, ts2 * TS:(ts2 + 1) * TS],
            start=True, stop=True),
            waits=[("dv", v_oh), ("dv", p4_wait)])
        st["oh_used"] = ctx.at("pe")
        v_cp = op("vector", lambda ee, slot=slot: ee.tensor_copy(
            sb["wbs"][:, (slot % 2) * TS:(slot % 2) * TS + TS], ps[4][:, 0:TS]),
            waits=[("pe", v_mm)], small=True)
        st["wbs_ready"][slot % 2] = v_cp
        return v_cp

    def gate_up(e, ts):
        groups = []
        for oc in range(8):
            bank = oc // 2
            col0 = (oc % 2) * TS
            for kc in range(NCH):
                lh = whi[:, kc * 1024 + oc * 128:kc * 1024 + (oc + 1) * 128]
                ll = wlo[:, kc * 1024 + oc * 128:kc * 1024 + (oc + 1) * 128]
                rh = Bhi3[:, kc, ts * TS:(ts + 1) * TS]
                rl = Blo3[:, kc, ts * TS:(ts + 1) * TS]
                for pi, (lt, rt) in enumerate(((lh, rh), (lh, rl), (ll, rh))):
                    w = []
                    if oc % 2 == 0 and kc == 0 and pi == 0:
                        w.append(("gu", st["gu_rdy"]))
                        if st["gu_bank_free"][bank] is not None:
                            w.append(st["gu_bank_free"][bank])
                    last = (oc % 2 == 1 and kc == NCH - 1 and pi == 2)
                    op("tensor", lambda ee, lt=lt, rt=rt, bank=bank, col0=col0,
                       first=(oc % 2 == 0 and kc == 0 and pi == 0), last=last: ee.matmul(
                        ps[bank][:, col0:col0 + TS], lt, rt,
                        start=first, stop=last, skip_group_check=True),
                        waits=w, inc=last)
            if oc % 2 == 1:
                groups.append(ctx.at("pe"))
        return groups

    def su_evac(e, ts, v_groups):
        su0 = A_SU[ts % 2]
        v_silu = [0] * 4
        for ic in range(4):
            gbank, gcol = ic // 2, (ic % 2) * TS
            w = [("pe", v_groups[ic // 2])]
            if ic == 0:
                w.append(("pe", st["su_free"][ts % 2]))
                if "su_scr_free" in st:
                    w.append(("dv", st["su_scr_free"]))
            v_silu[ic] = op("scalar", lambda ee, ic=ic, gbank=gbank, gcol=gcol:
                            ee.activation(
                                sus[:, ic * TS:(ic + 1) * TS],
                                ps[gbank][:, gcol:gcol + TS], AF.Silu,
                                bias=0.0, scale=1.0),
                            waits=w)
        for half in range(2):
            ubank = 2 + half
            op("vector", lambda ee, half=half, ubank=ubank: ee.scalar_tensor_tensor(
                sus[:, half * 512:(half + 1) * 512],
                ps[ubank][:, 0:512], 1.0,
                sus[:, half * 512:(half + 1) * 512],
                ALU.mult, ALU.mult),
                waits=[("ac", v_silu[half * 2 + 1]), ("pe", v_groups[2 + half])])
        op("vector", lambda ee, ts=ts: ee.tensor_tensor(
            sus[:].rearrange("p (i t) -> p i t", i=4),
            sus[:].rearrange("p (i t) -> p i t", i=4),
            sb["wbs"][:, (ts % 2) * TS:(ts % 2) * TS + TS]
            .unsqueeze(1).broadcast_to([128, 4, TS]), ALU.mult),
            waits=[("dv", st["wbs_ready"][(e * NTS + ts) % 2])])
        suh = A[:, su0:su0 + 512].bitcast(BF16)
        sul = A[:, su0 + 512:su0 + 1024].bitcast(BF16)
        op("vector", lambda ee: ee.tensor_copy(suh, sus[:]))
        op("vector", lambda ee: ee.tensor_tensor(sul, sus[:], suh, ALU.subtract))
        vd, va = ctx.at("dv"), ctx.at("ac")
        st["gu_bank_free"] = [("ac", v_silu[1]), ("ac", v_silu[3]), ("dv", vd), ("dv", vd)]
        st["su_scr_free"] = vd
        return vd

    def down(e, ts, v_su):
        su0 = A_SU[ts % 2]
        suh = A[:, su0:su0 + 512].bitcast(BF16)
        sul = A[:, su0 + 512:su0 + 1024].bitcast(BF16)
        for oc in range(8):
            bank = 4 + oc // 2
            col0 = (oc % 2) * TS
            for ic in range(4):
                lh = dhi[:, ic * 1024 + oc * 128:ic * 1024 + (oc + 1) * 128]
                ll = dlo[:, ic * 1024 + oc * 128:ic * 1024 + (oc + 1) * 128]
                rh = suh[:, ic * TS:(ic + 1) * TS]
                rl = sul[:, ic * TS:(ic + 1) * TS]
                for pi, (lt, rt) in enumerate(((lh, rh), (lh, rl), (ll, rh))):
                    w = []
                    if oc % 2 == 0 and ic == 0 and pi == 0:
                        w.append(("dn", st["dn_rdy"]))
                        w.append(("dv", v_su))
                        w.append(("dv", st["dn_bank_free"][bank - 4]))
                    last = (oc % 2 == 1 and ic == 3 and pi == 2)
                    op("tensor", lambda ee, lt=lt, rt=rt, bank=bank, col0=col0,
                       first=(oc % 2 == 0 and ic == 0 and pi == 0), last=last: ee.matmul(
                        ps[bank][:, col0:col0 + TS], lt, rt,
                        start=first, stop=last, skip_group_check=True),
                        waits=w, inc=last)
        v_mm = ctx.at("pe")
        st["su_free"][ts % 2] = v_mm
        v_evac_p4 = None
        for oc in range(8):
            bank = 4 + oc // 2
            col0 = (oc % 2) * TS
            op("vector", lambda ee, oc=oc, ts=ts, bank=bank, col0=col0: ee.tensor_tensor(
                C[:, oc * 2048 + ts * TS:oc * 2048 + (ts + 1) * TS],
                C[:, oc * 2048 + ts * TS:oc * 2048 + (ts + 1) * TS],
                ps[bank][:, col0:col0 + TS], ALU.add),
                waits=[("pe", v_mm)])
            if oc == 1:
                v_evac_p4 = ctx.at("dv")
        v = ctx.at("dv")
        st["dn_bank_free"] = [v, v, v, v]
        return v_evac_p4

    v_cp0 = build_wb(0, v_dv0)
    v_cp1 = build_wb(1, v_cp0)
    st["dn_bank_free"][0] = v_cp1
    pend_down = None
    v_gu_end_prev = 0
    v_dn_end_prev = 0
    for e in range(E_R):
        for ts in range(NTS):
            slot = e * NTS + ts
            v_evac_p4 = None
            if pend_down is not None:
                pd_e, pd_ts, pd_su = pend_down
                v_evac_p4 = down(pd_e, pd_ts, pd_su)
                if pd_ts == NTS - 1:
                    v_dn_end_prev = ctx.at("pe")
            if pend_down is not None and pend_down[1] == NTS - 1 and pend_down[0] + 1 < E_R:
                st["dn_rdy"] = op("sync", lambda ee, e=e: ee.dma_start(
                    out=A[:, 8192:12288].bitcast(BF16), in_=dr["dn_hl"][e]),
                    waits=[("pe", v_dn_end_prev)], dma=True, sem="dn")
            v_groups = gate_up(e, ts)
            v_gu = v_groups[-1]
            if e + 1 < E_R and ts == NTS - 1:
                st["gu_rdy"] = op("sync", lambda ee, e=e: ee.dma_start(
                    out=A[:, 0:8192].bitcast(BF16), in_=dr["gu_hl"][e + 1]),
                    waits=[("pe", v_gu)], dma=True, sem="gu")
            v_su = su_evac(e, ts, v_groups)
            if slot + 2 < NSLOT:
                v_wbcp = build_wb(slot + 2,
                                  v_evac_p4 if v_evac_p4 is not None else st["dn_bank_free"][0])
                st["dn_bank_free"][0] = v_wbcp
            pend_down = (e, ts, v_su)
    down(*pend_down)


def out_phase(ctx, sb, ps, dr, v_ident, n_tiles=NTT):
    op = ctx.op
    A, C = sb["A"], sb["C"]
    NOB = 4
    ost_free = [0] * NOB
    bank_free = [0, 0]
    for tt in range(n_tiles):
        buf = tt % NOB
        o0 = buf * 1024
        for half in range(2):
            bank = half
            for k in range(4):
                cchunk = half * 4 + k
                w = [("dv", v_ident)]
                if k == 0:
                    w.append(("dv", bank_free[bank]))
                op("tensor", lambda e, tt=tt, cchunk=cchunk, bank=bank, k=k: e.transpose(
                    ps[bank][:, k * 128:(k + 1) * 128],
                    C[:, cchunk * 2048 + tt * 128:cchunk * 2048 + (tt + 1) * 128],
                    sb["ident"][:]),
                    waits=w, inc=(k == 3))
            v_tp = ctx.at("pe")
            op("vector", lambda e, half=half, bank=bank, o0=o0: e.tensor_copy(
                A[:, o0 + half * 512:o0 + (half + 1) * 512], ps[bank][:, 0:512]),
                waits=[("pe", v_tp), ("gp", ost_free[buf])])
            bank_free[bank] = ctx.at("dv")
        v_cp = ctx.at("dv")
        sname = "st0" if tt % 2 == 0 else "st1"
        v_st = op("gpsimd", lambda e, tt=tt, o0=o0: e.dma_start(
            out=dr["out"][tt * 128:(tt + 1) * 128, :], in_=A[:, o0:o0 + 1024]),
            waits=[("dv", v_cp)], dma=True, sem=sname)
        ost_free[buf] = op("gpsimd", lambda e: e.engine_nop(),
                           waits=[(sname, v_st)])


def stage_inputs(L0_attn_qkv, L0_attn_wo, L0_ffn_classifier_weight,
                 L0_ffn_e_score_correction_bias, L0_ffn_gate_up_proj,
                 L0_ffn_down_proj):
    f32 = np.float32
    qkvT = np.ascontiguousarray(np.asarray(L0_attn_qkv, dtype=f32).T)
    q5 = qkvT.reshape(8, 128, 3, 8, 128)
    qkv_st = np.ascontiguousarray(q5.transpose(3, 1, 0, 2, 4)).reshape(NHP, 128, 3072)
    woT = np.ascontiguousarray(np.asarray(L0_attn_wo, dtype=f32).T)
    wo_st = np.ascontiguousarray(
        woT.reshape(8, 128, 8, 128).transpose(1, 0, 2, 3)).reshape(128, 8192)
    wcT = np.ascontiguousarray(np.asarray(L0_ffn_classifier_weight, dtype=f32).T)
    wc_st = np.ascontiguousarray(wcT.reshape(8, 128, E_T).transpose(1, 0, 2)).reshape(128, 320)
    bias_st = np.ascontiguousarray(
        np.broadcast_to(np.asarray(L0_ffn_e_score_correction_bias, dtype=f32)[None, :],
                        (128, E_T)))
    gu = np.asarray(L0_ffn_gate_up_proj, dtype=f32)[:E_R]
    guT = gu.transpose(0, 2, 1)
    gu_st = np.ascontiguousarray(
        guT.reshape(E_R, 8, 128, 1024).transpose(0, 2, 1, 3)).reshape(E_R, 128, 8192)
    dn = np.asarray(L0_ffn_down_proj, dtype=f32)
    dnT = dn.transpose(0, 2, 1)
    dn_st = np.ascontiguousarray(
        dnT.reshape(E_R, 4, 128, 1024).transpose(0, 2, 1, 3)).reshape(E_R, 128, 4096)
    mask_st = build_mask_host()
    return dict(qkv_st=qkv_st, wo_st=wo_st, wc_st=wc_st, bias_st=bias_st,
                gu_st=gu_st, dn_st=dn_st, mask_st=mask_st)


def kernel(L0_attn_x, L0_attn_qkv, L0_attn_wo, L0_ffn_classifier_weight,
           L0_ffn_e_score_correction_bias, L0_ffn_gate_up_proj, L0_ffn_down_proj):
    x = np.asarray(L0_attn_x, dtype=np.float32)
    weights = stage_inputs(L0_attn_qkv, L0_attn_wo, L0_ffn_classifier_weight,
                           L0_ffn_e_score_correction_bias, L0_ffn_gate_up_proj,
                           L0_ffn_down_proj)
    nc = build_program()
    in_maps = []
    for c in range(8):
        m = dict(weights)
        m["x"] = np.ascontiguousarray(x[c])
        in_maps.append(m)
    res = run_bass_kernel_spmd(nc, in_maps, core_ids=list(range(8)))
    out = np.stack([res.results[c]["out"] for c in range(8)], axis=0)
    return out.astype(np.float32)


# version 11
# speedup vs baseline: 1.1096x; 1.1096x over previous
import contextlib
import numpy as np
import ml_dtypes
import concourse.bass as bass
import concourse.mybir as mybir
from concourse.bass_utils import run_bass_kernel_spmd

F32 = mybir.dt.float32
BF16 = mybir.dt.bfloat16
AF = mybir.ActivationFunctionType
ALU = mybir.AluOpType
AX = mybir.AxisListType

T = 2048
DM = 1024
NH = 16
DH = 64
NHP = 8
NCH = 8
NTT = 16
E_R, E_T, TOPK = 32, 40, 6
SCALE = 2.5
EPS = 1e-5
NEG = -1e30
BIGI = 65536.0
BIG2 = 1e30
WIN, DIL = 128, 4

ENGINES = ("tensor", "vector", "scalar", "sync", "gpsimd")
ESEM = {"tensor": "pe", "vector": "dv", "scalar": "ac", "sync": "sy", "gpsimd": "gp"}

DEBUG = 0
STOP_AFTER = None

C_QW = 0
C_QF = 3072
C_KF = 5120
C_VT = 7168
C_PB = 9216
C_PTS = 11264
C_VS = 11776
C_MSK = 13312
C_XT0, C_XT1, C_LNT = 9216, 10240, 11264
C_WOW = 0
C_XL0, C_XL1 = 8192, 12288
C_LNT2 = 0
C_SC = 0
C_SEL = 640
C_WFL = 1280
C_EQ = 1920
C_MI = 2560
C_T40 = 3200
C_WCAT = 3264
A_WGU = 0
A_WD = 8192
A_SU = (12288, 13312)
A_SUS = 14336
A_OST = (0, 1024)
CV_STG = 14336

N_MASK_TILES = sum(min(3, q + 1) for q in range(NTT)) + 1
T4_COL = (N_MASK_TILES - 1) * 128


def _boundary_index(qi, jb):
    cum = sum(min(3, q + 1) for q in range(qi))
    return cum + (jb - max(0, qi - 2))


def build_mask_host():
    out = np.zeros((128, N_MASK_TILES * 128), dtype=np.float32)
    col = 0
    for qi in range(NTT):
        i = qi * 128 + np.arange(128)[:, None]
        for jb in range(max(0, qi - 2), qi + 1):
            j = jb * 128 + np.arange(128)[None, :]
            win = (j <= i) & (j >= i - WIN + 1)
            base = i - WIN - DIL
            dil = (j <= base) & ((base - j) % DIL == 0)
            out[:, col:col + 128] = np.where(win | dil, 0.0, NEG)
            col += 128
    p = np.arange(128)[:, None]
    jj = np.arange(128)[None, :]
    out[:, col:col + 128] = np.where((p - jj) % DIL == 0, 0.0, NEG)
    return out.astype(ml_dtypes.bfloat16)


class Ctx:

    def __init__(self, blk, sems, eng_name, eng):
        self.blk = blk
        self.sems = sems
        self.counts = {k: 0 for k in sems}
        self.eng_name = eng_name
        self.eng = eng
        self.pending_small = {}
        self.force_small = False

    def at(self, s):
        return self.counts[s]

    def op(self, engine, emit, waits=(), dma=False, sem=None, inc=True, small=False):
        if self.force_small and engine == "vector":
            small = True
        semname = (sem or ESEM[engine]) if inc else None
        amt = 16 if dma else 1
        tgt = None
        if semname is not None:
            self.counts[semname] += amt
            tgt = self.counts[semname]
        psmall = self.pending_small.get(engine)
        self.pending_small[engine] = (semname, tgt) if (small and semname) else None
        if self.eng_name == engine:
            if psmall is not None:
                self.eng.wait_ge(self.sems[psmall[0]], psmall[1])
            for s, v in waits:
                if v > 0:
                    self.eng.wait_ge(self.sems[s], v)
            instr = emit(self.eng)
            if semname is not None:
                instr.then_inc(self.sems[semname], amt)
        return tgt

    def barrier(self):
        snap = dict(self.counts)
        own = ESEM[self.eng_name]
        for s, v in snap.items():
            if v > 0 and s != own:
                self.eng.wait_ge(self.sems[s], v)

    def final_wait(self, engine, semname, value):
        if self.eng_name == engine:
            self.eng.wait_ge(self.sems[semname], value)


def build_program():
    nc = bass.Bass("TRN2", target_bir_lowering=False)
    dr = {}
    dr["x"] = nc.dram_tensor("x", [T, DM], F32, kind="ExternalInput").ap()
    dr["qkv_st"] = nc.dram_tensor("qkv_st", [NHP, 128, 3072], F32, kind="ExternalInput").ap()
    dr["wo_st"] = nc.dram_tensor("wo_st", [128, 8192], F32, kind="ExternalInput").ap()
    dr["wc_st"] = nc.dram_tensor("wc_st", [128, 320], F32, kind="ExternalInput").ap()
    dr["bias_st"] = nc.dram_tensor("bias_st", [128, E_T], F32, kind="ExternalInput").ap()
    dr["gu_st"] = nc.dram_tensor("gu_st", [E_R, 128, 8192], F32, kind="ExternalInput").ap()
    dr["dn_st"] = nc.dram_tensor("dn_st", [E_R, 128, 4096], F32, kind="ExternalInput").ap()
    dr["mask_st"] = nc.dram_tensor("mask_st", [128, N_MASK_TILES * 128], BF16,
                                   kind="ExternalInput").ap()
    dr["out"] = nc.dram_tensor("out", [T, DM], F32, kind="ExternalOutput").ap()
    dr["gu_hl"] = nc.dram_tensor("gu_hl", [E_R, 128, 16384], BF16, kind="Internal").ap()
    dr["dn_hl"] = nc.dram_tensor("dn_hl", [E_R, 128, 8192], BF16, kind="Internal").ap()
    if DEBUG:
        for nm, shape in [("dbg_a", [128, 16384]), ("dbg_b", [128, 16384]),
                          ("dbg_c", [128, 16384]), ("dbg_wfm", [41, 2048]),
                          ("dbg_s", [128, 64])]:
            dr[nm] = nc.dram_tensor(nm, shape, F32, kind="ExternalOutput").ap()

    with contextlib.ExitStack() as est:
        sb = {}
        for nm, shape in [("A", [128, 16384]), ("B", [128, 16384]), ("C", [128, 16384]),
                          ("ident", [128, 128]), ("oh", [41, 128]), ("ohb", [41, 128]),
                          ("wcw", [128, 320]), ("bias40", [128, E_T]), ("iotaB", [128, E_T]),
                          ("wfm", [41, 2048]), ("stg", [128, 512]), ("stats", [128, 12]),
                          ("mv", [128, 2]), ("sig", [128, 1]), ("rinv", [128, 1]),
                          ("negmax", [128, 2]), ("ell", [128, 2]), ("linv", [128, 2]),
                          ("gneg", [128, NTT]), ("gsinv", [128, NTT]), ("zwt", [128, NTT]),
                          ("rmx", [128, NTT]), ("rmn", [128, NTT]),
                          ("wbs", [128, 512])] + ([("dbgs", [128, 64])] if DEBUG else []):
            sb[nm] = est.enter_context(nc.sbuf_tensor(nm, shape, F32))
        ps = [est.enter_context(nc.psum_tensor(f"ps{i}", [128, 512], F32)) for i in range(8)]
        semnames = ["pe", "dv", "ac", "sy", "gp",
                    "xs0", "xs1", "cq", "cst", "gu", "dn", "st0", "st1"]
        sems = {s: est.enter_context(nc.semaphore(s)) for s in semnames}
        blk = est.enter_context(nc.Block())

        for en in ENGINES:
            def section(eng, en=en):
                ctx = Ctx(blk, sems, en, eng)
                ctx.nc = nc
                emit_all(ctx, dr, sb, ps)
            getattr(blk, en)(section)
    return nc


def emit_all(ctx, dr, sb, ps):
    op = ctx.op
    A, B, C = sb["A"], sb["B"], sb["C"]
    ident = sb["ident"]
    masksv = C[:, C_MSK:C_MSK + N_MASK_TILES * 64].bitcast(BF16)
    A8 = A[:].rearrange("p (c t) -> p c t", c=8)
    B8 = B[:].rearrange("p (c t) -> p c t", c=8)
    C8 = C[:].rearrange("p (c t) -> p c t", c=8)

    v = op("gpsimd", lambda e: e.iota(ident[:], pattern=[[-1, 128]], base=0,
                                      channel_multiplier=1, allow_small_or_imprecise_dtypes=True))
    op("vector", lambda e: e.tensor_scalar(ident[:], ident[:], 0.0, None, ALU.is_equal),
       waits=[("gp", v)])
    v_ident = ctx.at("dv")
    v = op("gpsimd", lambda e: e.iota(sb["iotaB"][:], pattern=[[1, E_T]], base=0,
                                      channel_multiplier=0, allow_small_or_imprecise_dtypes=True))
    op("vector", lambda e: e.tensor_scalar_add(sb["iotaB"][:], sb["iotaB"][:], BIGI),
       waits=[("gp", v)])
    op("gpsimd", lambda e: e.iota(sb["ohb"][:], pattern=[[0, 128]], base=0,
                                  channel_multiplier=1, allow_small_or_imprecise_dtypes=True))
    v_ohb = ctx.at("gp")
    op("sync", lambda e: e.dma_start(out=sb["wcw"][:], in_=dr["wc_st"]), dma=True, sem="cst")
    op("sync", lambda e: e.dma_start(out=sb["bias40"][:], in_=dr["bias_st"]), dma=True, sem="cst")
    v_mask_ld = op("sync", lambda e: e.dma_start(out=masksv, in_=dr["mask_st"]), dma=True, sem="cst")
    v_qw_ld = op("sync", lambda e: e.dma_start(out=C[:, C_QW:C_QW + 3072],
                                               in_=dr["qkv_st"][0]), dma=True, sem="cq")

    def stopped(phase):
        if STOP_AFTER != phase:
            return False
        if DEBUG:
            ctx.barrier()
            for nm, src in [("dbg_a", sb["A"][:]), ("dbg_b", sb["B"][:]),
                            ("dbg_c", sb["C"][:]), ("dbg_wfm", sb["wfm"][:]),
                            ("dbg_s", sb["dbgs"][:])]:
                op("gpsimd", lambda e, nm=nm, src=src: e.dma_start(out=dr[nm], in_=src),
                   dma=True, sem="st0")
            ctx.final_wait("gpsimd", "st0", ctx.at("st0"))
        return True

    cvst = {"out1": 0}

    def convert_experts(es):
        C7 = sb["B"]
        stg_in = C7[:, CV_STG:CV_STG + 1024]
        stg_hi = C7[:, CV_STG + 1024:CV_STG + 1536].bitcast(BF16)
        stg_lo = C7[:, CV_STG + 1536:CV_STG + 2048].bitcast(BF16)
        for e in es:
            for q in range(12):
                if q < 8:
                    src = dr["gu_st"][e][:, q * 1024:(q + 1) * 1024]
                    dhi = dr["gu_hl"][e][:, q * 1024:(q + 1) * 1024]
                    dlo = dr["gu_hl"][e][:, 8192 + q * 1024:8192 + (q + 1) * 1024]
                else:
                    qq = q - 8
                    src = dr["dn_st"][e][:, qq * 1024:(qq + 1) * 1024]
                    dhi = dr["dn_hl"][e][:, qq * 1024:(qq + 1) * 1024]
                    dlo = dr["dn_hl"][e][:, 4096 + qq * 1024:4096 + (qq + 1) * 1024]
                v_in = op("sync", lambda ee, src=src: ee.dma_start(out=stg_in, in_=src),
                          waits=[("dn", cvst["out1"])], dma=True, sem="gu")
                op("gpsimd", lambda ee: ee.tensor_copy(stg_hi, stg_in),
                   waits=[("gu", v_in), ("dn", cvst["out1"])])
                v_sp = op("gpsimd", lambda ee: ee.tensor_tensor(
                    stg_lo, stg_in, stg_hi, ALU.subtract))
                op("sync", lambda ee, dhi=dhi: ee.dma_start(
                    out=dhi, in_=stg_hi), waits=[("gp", v_sp)], dma=True, sem="dn")
                cvst["out1"] = op("sync", lambda ee, dlo=dlo: ee.dma_start(
                    out=dlo, in_=stg_lo), dma=True, sem="dn")

    with ctx.nc.named_scope("p1_ln1"):
        ln_to_fm(ctx, sb, ps, dst8=A8, src_kind="hbm", dr=dr, v_ident=v_ident)
    convert_experts(range(0, 2))
    ctx.barrier()
    if stopped("ln1"):
        return

    st = {"sbank_free": [None] * 4,
          "p4_free": 0,
          "obank_free": [0, 0],
          "pts_free": 0,
          "p_free": [0, 0],
          "exp_done": [0, 0],
          "qkv_bank_free": [None] * 4,
          "vs_free": 0, "vt_free": 0,
          "qw_ld": v_qw_ld}
    with ctx.nc.named_scope("p2_attn"):
        for hp in range(NHP):
            st["conv_dn"] = cvst["out1"]
            attn_headpair(ctx, sb, ps, dr, hp, st, v_ident, v_mask_ld)
            if hp < 6:
                convert_experts(range(2 + hp * 5, 2 + (hp + 1) * 5))
    ctx.barrier()
    if stopped("attn"):
        return

    v_wo_ld = op("sync", lambda e: e.dma_start(out=C[:, C_WOW:C_WOW + 8192],
                                               in_=dr["wo_st"]), dma=True, sem="cst")
    with ctx.nc.named_scope("p3_wo"):
        wo_phase(ctx, sb, ps, dr, v_wo_ld, v_ident)
    ctx.barrier()
    if stopped("wo"):
        return

    with ctx.nc.named_scope("p4_ln2"):
        ln_to_fm(ctx, sb, ps, dst8=B8, src_kind="A", dr=dr, v_ident=v_ident)
    ctx.barrier()
    if stopped("ln2"):
        return

    with ctx.nc.named_scope("p5_route"):
        routing(ctx, sb, ps, v_ident)
    ctx.barrier()
    if stopped("route"):
        return

    with ctx.nc.named_scope("p6_init"):
        v_accinit_tp = acc_init(ctx, sb, ps, v_ident, v_ohb)
    Bhi_full = B[:, 0:8192].bitcast(BF16)
    Blo_full = B[:, 8192:16384].bitcast(BF16)
    op("vector", lambda e: e.tensor_copy(A[:], B[:]), waits=[("pe", v_accinit_tp)])
    op("vector", lambda e: e.tensor_copy(Bhi_full, A[:]))
    v_conv = op("vector", lambda e: e.tensor_tensor(Blo_full, A[:], Bhi_full, ALU.subtract))
    v_gu_ld = op("sync", lambda e: e.dma_start(
        out=A[:, 0:8192].bitcast(BF16), in_=dr["gu_hl"][0]),
        waits=[("pe", v_accinit_tp), ("dv", v_conv)], dma=True, sem="gu")
    v_dn_ld = op("sync", lambda e: e.dma_start(
        out=A[:, 8192:12288].bitcast(BF16), in_=dr["dn_hl"][0]),
        waits=[("pe", v_accinit_tp)], dma=True, sem="dn")
    ctx.barrier()
    if stopped("init"):
        return

    with ctx.nc.named_scope("p7_moe"):
        moe_experts(ctx, sb, ps, dr, v_gu_ld, v_dn_ld)
    ctx.barrier()
    if stopped("moe"):
        return

    if DEBUG:
        vs = []
        vs.append(op("gpsimd", lambda e: e.dma_start(out=dr["dbg_a"], in_=A[:]), dma=True, sem="st0"))
        vs.append(op("gpsimd", lambda e: e.dma_start(out=dr["dbg_b"], in_=B[:]), dma=True, sem="st0"))
        vs.append(op("gpsimd", lambda e: e.dma_start(out=dr["dbg_c"], in_=C[:]), dma=True, sem="st0"))
        vs.append(op("gpsimd", lambda e: e.dma_start(out=dr["dbg_wfm"], in_=sb["wfm"][:]),
                     dma=True, sem="st0"))

    n_tiles = int(STOP_AFTER[3:]) if (STOP_AFTER or "").startswith("out") else NTT
    with ctx.nc.named_scope("p8_out"):
        out_phase(ctx, sb, ps, dr, v_ident, n_tiles)
    ctx.final_wait("gpsimd", "st0", ctx.at("st0"))
    ctx.final_wait("gpsimd", "st1", ctx.at("st1"))


def ln_to_fm(ctx, sb, ps, dst8, src_kind, dr, v_ident):
    op = ctx.op
    C = sb["C"]
    A = sb["A"]
    last_lnt_reader = 0
    last_xt_reader = [0, 0]
    bank_free = [0, 0]
    lnt_off = C_LNT if src_kind == "hbm" else C_LNT2
    for tt in range(NTT):
        buf = tt % 2
        if src_kind == "hbm":
            xt_off = C_XT0 if buf == 0 else C_XT1
            xsem = "xs0" if buf == 0 else "xs1"
            v_x = op("gpsimd", lambda e, tt=tt, xt_off=xt_off: e.dma_start(
                out=C[:, xt_off:xt_off + 1024],
                in_=dr["x"][tt * 128:(tt + 1) * 128, :]),
                waits=[("dv", last_xt_reader[buf])], dma=True, sem=xsem)
            src = C[:, xt_off:xt_off + 1024]
            w0 = [(xsem, v_x)]
        else:
            src = A[:, tt * 1024:(tt + 1) * 1024]
            w0 = []
        v_red = op("vector", lambda e, src=src: e.reduce_sum(sb["mv"][:, 0:1], src, axis=AX.X),
                   waits=w0, small=True)
        v_sq = op("scalar", lambda e, src=src, lnt_off=lnt_off: e.activation(
            C[:, lnt_off:lnt_off + 1024], src, AF.Square, bias=0.0, scale=1.0,
            accum_out=sb["mv"][:, 1:2]),
            waits=w0 + [("pe", last_lnt_reader), ("dv", v_red)])
        op("vector", lambda e: e.tensor_scalar_mul(sb["mv"][:, 0:1], sb["mv"][:, 0:1], 1.0 / 1024.0),
           waits=[("ac", v_sq)], small=True)
        op("vector", lambda e: e.tensor_tensor(sb["stats"][:, 0:1], sb["mv"][:, 0:1],
                                               sb["mv"][:, 0:1], ALU.mult), small=True)
        op("vector", lambda e: e.tensor_scalar(sb["sig"][:], sb["mv"][:, 1:2], 1.0 / 1024.0,
                                               sb["stats"][:, 0:1], ALU.mult, ALU.subtract), small=True)
        v_sig = op("vector", lambda e: e.tensor_scalar_add(sb["sig"][:], sb["sig"][:], EPS),
                   small=True)
        v_sqrt = op("scalar", lambda e: e.activation(sb["sig"][:], sb["sig"][:], AF.Sqrt,
                                                     bias=0.0, scale=1.0),
                    waits=[("dv", v_sig)])
        op("vector", lambda e: e.reciprocal(sb["rinv"][:], sb["sig"][:]), waits=[("ac", v_sqrt)],
           small=True)
        if DEBUG and src_kind == "hbm":
            op("vector", lambda e, tt=tt: e.tensor_copy(sb["dbgs"][:, tt*4:tt*4+2], sb["mv"][:]))
            op("vector", lambda e, tt=tt: e.tensor_copy(sb["dbgs"][:, tt*4+2:tt*4+3], sb["sig"][:]))
            op("vector", lambda e, tt=tt: e.tensor_copy(sb["dbgs"][:, tt*4+3:tt*4+4], sb["rinv"][:]))
        v_ln = op("vector", lambda e, src=src, lnt_off=lnt_off: e.tensor_scalar(
            C[:, lnt_off:lnt_off + 1024], src,
            sb["mv"][:, 0:1], sb["rinv"][:], ALU.subtract, ALU.mult),
            waits=[("pe", last_lnt_reader)])
        if src_kind == "hbm":
            last_xt_reader[buf] = v_ln
        for half in range(2):
            bank = half
            for k in range(4):
                cchunk = half * 4 + k
                w = [("dv", v_ln), ("dv", v_ident)]
                if k == 0:
                    w.append(("ac", bank_free[bank]))
                op("tensor", lambda e, cchunk=cchunk, bank=bank, k=k, lnt_off=lnt_off: e.transpose(
                    ps[bank][:, k * 128:(k + 1) * 128],
                    C[:, lnt_off + cchunk * 128:lnt_off + (cchunk + 1) * 128],
                    sb["ident"][:]),
                    waits=w, inc=(k == 3))
            v_tp = ctx.at("pe")
            last_lnt_reader = v_tp
            op("scalar", lambda e, half=half, tt=tt, bank=bank: e.copy(
                dst8[:, half * 4:half * 4 + 4, tt * 128:(tt + 1) * 128],
                ps[bank][:, 0:512].rearrange("p (c t) -> p c t", c=4)),
                waits=[("pe", v_tp)])
            bank_free[bank] = ctx.at("ac")


def attn_headpair(ctx, sb, ps, dr, hp, st, v_ident, v_mask_ld):
    op = ctx.op
    A, C = sb["A"], sb["C"]
    A8 = A[:].rearrange("p (c t) -> p c t", c=8)

    bank_rot = 0
    for m in range(3):
        for ts in range(4):
            bank = bank_rot % 4
            bank_rot += 1
            for kc in range(NCH):
                w = [("cq", st["qw_ld"])]
                if kc == 0 and st["qkv_bank_free"][bank] is not None:
                    w.append(st["qkv_bank_free"][bank])
                op("tensor", lambda e, m=m, ts=ts, kc=kc, bank=bank: e.matmul(
                    ps[bank][:, 0:512],
                    C[:, C_QW + (kc * 3 + m) * 128:C_QW + (kc * 3 + m) * 128 + 128],
                    A8[:, kc, ts * 512:(ts + 1) * 512],
                    start=(kc == 0), stop=(kc == NCH - 1)),
                    waits=w, inc=(kc == NCH - 1))
            v_mm = ctx.at("pe")
            if m < 2:
                dst_off = C_QF if m == 0 else C_KF
                op("vector", lambda e, dst_off=dst_off, ts=ts, bank=bank: e.tensor_copy(
                    C[:, dst_off + ts * 512:dst_off + (ts + 1) * 512], ps[bank][:, 0:512]),
                    waits=[("pe", v_mm)])
                st["qkv_bank_free"][bank] = ("dv", ctx.at("dv"))
            else:
                v_vs = op("scalar", lambda e, bank=bank: e.copy(
                    C[:, C_VS:C_VS + 512], ps[bank][:, 0:512]),
                    waits=[("pe", v_mm), ("pe", st["vs_free"])])
                st["qkv_bank_free"][bank] = ("ac", v_vs)
                for k in range(4):
                    w = [("ac", v_vs), ("dv", v_ident)]
                    if k == 0:
                        w.append(("ac", st["p4_free"]))
                        w.append(("dv", st["vt_free"]))
                    op("tensor", lambda e, k=k: e.transpose(
                        ps[4][:, k * 128:(k + 1) * 128],
                        C[:, C_VS + k * 128:C_VS + (k + 1) * 128], sb["ident"][:]),
                        waits=w, inc=(k == 3))
                v_tp = ctx.at("pe")
                st["vs_free"] = v_tp
                op("vector", lambda e, ts=ts: e.tensor_copy(
                    C[:, C_VT + ts * 512:C_VT + (ts + 1) * 512], ps[4][:, 0:512]),
                    waits=[("pe", v_tp)])
                st["vt_free"] = ctx.at("dv")
    v_qkv_done = ctx.at("pe")
    for b in range(4):
        st["sbank_free"][b] = st["qkv_bank_free"][b]
    if hp + 1 < NHP:
        st["qw_ld"] = op("sync", lambda e, hp=hp: e.dma_start(
            out=C[:, C_QW:C_QW + 3072], in_=dr["qkv_st"][hp + 1]),
            waits=[("pe", v_qkv_done)], dma=True, sem="cq")

    def p_ap(par, lo, hi):
        if par == 0:
            return C[:, C_PB + lo:C_PB + hi]
        if hp == NHP - 1:
            return C[:, C_QW + lo:C_QW + hi]
        return sb["B"][:, (hp + 1) * 2048 + lo:(hp + 1) * 2048 + hi]

    def emit_ptpv(pend):
        head, qi, par, v_norm = pend
        h64 = head * 64
        jn = (qi + 1) * 128
        njc = (jn + 511) // 512
        first = True
        for jc in range(njc):
            w = min(512, jn - jc * 512)
            nk = w // 128
            for k in range(nk):
                wts = [("dv", v_norm), ("dv", v_ident)]
                if k == 0:
                    wts.append(("ac", st["p4_free"]))
                op("tensor", lambda e, jc=jc, k=k, par=par: e.transpose(
                    ps[4][:, k * 128:(k + 1) * 128],
                    p_ap(par, jc * 512 + k * 128, jc * 512 + (k + 1) * 128),
                    sb["ident"][:]),
                    waits=wts, inc=(k == nk - 1))
            v_pt = ctx.at("pe")
            st["p_free"][par] = v_pt
            v_ptc = op("scalar", lambda e, w=w: e.copy(
                C[:, C_PTS:C_PTS + w], ps[4][:, 0:w]),
                waits=[("pe", v_pt), ("pe", st["pts_free"])])
            st["p4_free"] = v_ptc
            for k in range(nk):
                wts = [("ac", v_ptc)]
                if first:
                    wts.append(("ac", st["obank_free"][head]))
                last = (jc == njc - 1 and k == nk - 1)
                op("tensor", lambda e, jc=jc, k=k, h64=h64, head=head, first=first,
                   last=last: e.matmul(
                    ps[5 + head][h64:h64 + 64, 0:128],
                    C[:, C_VT + (jc * 4 + k) * 128 + h64:C_VT + (jc * 4 + k) * 128 + h64 + 64],
                    C[:, C_PTS + k * 128:C_PTS + (k + 1) * 128],
                    start=first, stop=last,
                    tile_position=(0, h64)),
                    waits=wts, inc=last)
                first = False
        v_pv = ctx.at("pe")
        st["pts_free"] = v_pv
        op("scalar", lambda e, h64=h64, qi=qi, head=head: e.copy(
            sb["B"][h64:h64 + 64, hp * 2048 + qi * 128:hp * 2048 + (qi + 1) * 128],
            ps[5 + head][h64:h64 + 64, 0:128]),
            waits=[("pe", v_pv)])
        st["obank_free"][head] = ctx.at("ac")

    pend = None
    for head in range(2):
        h64 = head * 64
        for qi in range(NTT):
            it = head * NTT + qi
            par = it % 2
            jn = (qi + 1) * 128
            njc = (jn + 511) // 512
            s_done = []
            for jc in range(njc):
                w = min(512, jn - jc * 512)
                wts = [("dv", st["vt_free"])]
                if st["sbank_free"][jc] is not None:
                    wts.append(st["sbank_free"][jc])
                op("tensor", lambda e, jc=jc, w=w, qi=qi, h64=h64: e.matmul(
                    ps[jc][:, 0:w],
                    C[h64:h64 + 64, C_QF + qi * 128:C_QF + (qi + 1) * 128],
                    C[h64:h64 + 64, C_KF + jc * 512:C_KF + jc * 512 + w],
                    start=True, stop=True),
                    waits=wts)
                s_done.append(ctx.at("pe"))
            for jc in range(njc):
                w = min(512, jn - jc * 512)
                nblocks = w // 128
                jb0 = jc * 4
                npure = min(max(qi - 2 - jb0, 0), nblocks)
                waits = [("pe", s_done[jc]), ("pe", st["p_free"][par]),
                         ("cst", v_mask_ld)]
                if hp == NHP - 2 and par == 1:
                    waits.append(("dn", st.get("conv_dn", 0)))
                if npure > 0:
                    op("vector", lambda e, jc=jc, npure=npure, par=par: e.scalar_tensor_tensor(
                        p_ap(par, jc * 512, jc * 512 + npure * 128),
                        ps[jc][:, 0:npure * 128], 0.125,
                        masksv_t4(sb).unsqueeze(1).broadcast_to([128, npure, 128]),
                        ALU.mult, ALU.add),
                        waits=waits)
                    waits = []
                if npure < nblocks:
                    nb = nblocks - npure
                    i0 = _boundary_index(qi, jb0 + npure)
                    op("vector", lambda e, jc=jc, npure=npure, nb=nb, i0=i0, par=par:
                       e.scalar_tensor_tensor(
                        p_ap(par, jc * 512 + npure * 128, jc * 512 + (npure + nb) * 128),
                        ps[jc][:, npure * 128:(npure + nb) * 128], 0.125,
                        masksv_all(sb)[:, i0 * 128:(i0 + nb) * 128],
                        ALU.mult, ALU.add),
                        waits=waits)
                st["sbank_free"][jc] = ("dv", ctx.at("dv"))
            v_max = op("vector", lambda e, jn=jn, par=par: e.reduce_max(
                sb["negmax"][:, par:par + 1], p_ap(par, 0, jn), axis=AX.X, negate=True),
                waits=[("ac", st["exp_done"][par])], small=True)
            v_exp = op("scalar", lambda e, jn=jn, par=par: e.activation(
                p_ap(par, 0, jn), p_ap(par, 0, jn), AF.Exp,
                bias=sb["negmax"][:, par:par + 1], scale=1.0,
                accum_out=sb["ell"][:, par:par + 1]),
                waits=[("dv", v_max)])
            st["exp_done"][par] = v_exp
            op("vector", lambda e, par=par: e.reciprocal(
                sb["linv"][:, par:par + 1], sb["ell"][:, par:par + 1]),
               waits=[("ac", v_exp)], small=True)
            v_norm = op("vector", lambda e, jn=jn, par=par: e.tensor_scalar_mul(
                p_ap(par, 0, jn), p_ap(par, 0, jn), sb["linv"][:, par:par + 1]))
            if pend is not None:
                emit_ptpv(pend)
            pend = (head, qi, par, v_norm)
    emit_ptpv(pend)


def masksv_t4(sb):
    return sb["C"][:, C_MSK:C_MSK + N_MASK_TILES * 64].bitcast(BF16)[:, T4_COL:T4_COL + 128]


def masksv_all(sb):
    return sb["C"][:, C_MSK:C_MSK + N_MASK_TILES * 64].bitcast(BF16)


def wo_phase(ctx, sb, ps, dr, v_wo_ld, v_ident):
    op = ctx.op
    A, B, C = sb["A"], sb["B"], sb["C"]
    B8 = B[:].rearrange("p (c t) -> p c t", c=8)
    xl_reader = [0, 0]
    v_xl = [0, 0]
    mmbank_free = [0, 0]
    tpbank_free = [0, 0]
    stg_free = 0
    for ts in range(4):
        buf = ts % 2
        xoff = C_XL0 if buf == 0 else C_XL1
        xsem = "xs0" if buf == 0 else "xs1"
        v_xl[buf] = op("gpsimd", lambda e, ts=ts, xoff=xoff: e.dma_start(
            out=C[:, xoff:xoff + 4096].rearrange("p (a f) -> p a f", a=4),
            in_=dr["x"][ts * 512:(ts + 1) * 512, :].rearrange("(a p) f -> p a f", p=128)),
            waits=[("dv", xl_reader[buf])], dma=True, sem=xsem)
        for oc in range(NCH):
            bank = oc % 2
            for kc in range(NCH):
                w = [("cst", v_wo_ld)]
                if kc == 0:
                    w.append(("ac", mmbank_free[bank]))
                op("tensor", lambda e, oc=oc, kc=kc, ts=ts, bank=bank: e.matmul(
                    ps[bank][:, 0:512],
                    C[:, C_WOW + kc * 1024 + oc * 128:C_WOW + kc * 1024 + (oc + 1) * 128],
                    B8[:, kc, ts * 512:(ts + 1) * 512],
                    start=(kc == 0), stop=(kc == NCH - 1)),
                    waits=w, inc=(kc == NCH - 1))
            v_mm = ctx.at("pe")
            v_stg = op("scalar", lambda e, bank=bank: e.copy(
                sb["stg"][:], ps[bank][:, 0:512]),
                waits=[("pe", v_mm), ("pe", stg_free)])
            mmbank_free[bank] = v_stg
            tb = 2 + (oc % 2)
            for k in range(4):
                w = [("ac", v_stg), ("dv", v_ident)]
                if k == 0:
                    w.append(("dv", tpbank_free[tb - 2]))
                op("tensor", lambda e, k=k, tb=tb: e.transpose(
                    ps[tb][:, k * 128:(k + 1) * 128],
                    sb["stg"][:, k * 128:(k + 1) * 128], sb["ident"][:]),
                    waits=w, inc=(k == 3))
            v_tp = ctx.at("pe")
            stg_free = v_tp
            op("vector", lambda e, ts=ts, oc=oc, tb=tb, xoff=xoff: e.tensor_tensor(
                A[:].rearrange("p (a f) -> p a f", a=NTT)[:, ts * 4:ts * 4 + 4, oc * 128:(oc + 1) * 128],
                ps[tb][:, 0:512].rearrange("p (a f) -> p a f", a=4),
                C[:, xoff:xoff + 4096].rearrange("p (a f) -> p a f", a=4)[:, :, oc * 128:(oc + 1) * 128],
                ALU.add),
                waits=[("pe", v_tp), ("xs0" if buf == 0 else "xs1", v_xl[buf])])
            tpbank_free[tb - 2] = ctx.at("dv")
            xl_reader[buf] = ctx.at("dv")


def routing(ctx, sb, ps, v_ident):
    op = ctx.op
    ctx.force_small = True
    B, C = sb["B"], sb["C"]
    B8 = B[:].rearrange("p (c t) -> p c t", c=8)
    sc = C[:, C_SC:C_SC + 640]
    sel = C[:, C_SEL:C_SEL + 640]
    wfl = C[:, C_WFL:C_WFL + 640]
    eq = C[:, C_EQ:C_EQ + 640]
    mi = C[:, C_MI:C_MI + 640]
    t40 = C[:, C_T40:C_T40 + 40]
    wcat = C[:, C_WCAT:C_WCAT + 656]

    bank_free = [0, 0]
    for tt in range(NTT):
        bank = tt % 2
        for kc in range(NCH):
            w = []
            if kc == 0:
                w.append(("dv", bank_free[bank]))
            op("tensor", lambda e, tt=tt, kc=kc, bank=bank: e.matmul(
                ps[bank][:, 0:E_T],
                B8[:, kc, tt * 128:(tt + 1) * 128],
                sb["wcw"][:, kc * E_T:(kc + 1) * E_T],
                start=(kc == 0), stop=(kc == NCH - 1)),
                waits=w, inc=(kc == NCH - 1))
        v_mm = ctx.at("pe")
        op("vector", lambda e, tt=tt, bank=bank: e.tensor_copy(
            sc[:, tt * 40:(tt + 1) * 40], ps[bank][:, 0:E_T]),
            waits=[("pe", v_mm)])
        bank_free[bank] = ctx.at("dv")
    sc3 = sc.rearrange("p (g x) -> p g x", g=NTT)
    op("vector", lambda e: e.reduce_max(sb["gneg"][:], sc3, axis=AX.X, negate=True))
    v_gm = ctx.at("dv")
    for tt in range(NTT):
        op("scalar", lambda e, tt=tt: e.activation(
            sc[:, tt * 40:(tt + 1) * 40], sc[:, tt * 40:(tt + 1) * 40], AF.Exp,
            bias=sb["gneg"][:, tt:tt + 1], scale=1.0),
            waits=[("dv", v_gm)])
    v_exp = ctx.at("ac")
    op("vector", lambda e: e.reduce_sum(sb["gsinv"][:], sc3, axis=AX.X), waits=[("ac", v_exp)])
    op("vector", lambda e: e.reciprocal(sb["gsinv"][:], sb["gsinv"][:]))
    for tt in range(NTT):
        op("vector", lambda e, tt=tt: e.tensor_scalar_mul(
            sc[:, tt * 40:(tt + 1) * 40], sc[:, tt * 40:(tt + 1) * 40],
            sb["gsinv"][:, tt:tt + 1]))
    for tt in range(NTT):
        op("vector", lambda e, tt=tt: e.tensor_add(
            sel[:, tt * 40:(tt + 1) * 40], sc[:, tt * 40:(tt + 1) * 40], sb["bias40"][:]))
    op("vector", lambda e: e.memset(wfl, 0.0))
    for _k in range(TOPK):
        op("vector", lambda e: e.reduce_max(sb["rmx"][:],
                                            sel.rearrange("p (g x) -> p g x", g=NTT), axis=AX.X))
        for tt in range(NTT):
            op("vector", lambda e, tt=tt: e.tensor_scalar(
                eq[:, tt * 40:(tt + 1) * 40], sel[:, tt * 40:(tt + 1) * 40],
                sb["rmx"][:, tt:tt + 1], None, ALU.is_equal))
        for tt in range(NTT):
            op("vector", lambda e, tt=tt: e.scalar_tensor_tensor(
                mi[:, tt * 40:(tt + 1) * 40], eq[:, tt * 40:(tt + 1) * 40],
                -BIGI, sb["iotaB"][:], ALU.mult, ALU.add))
        op("vector", lambda e: e.tensor_reduce(sb["rmn"][:],
                                               mi.rearrange("p (g x) -> p g x", g=NTT),
                                               axis=AX.X, op=ALU.min))
        for tt in range(NTT):
            op("vector", lambda e, tt=tt: e.tensor_scalar(
                eq[:, tt * 40:(tt + 1) * 40], mi[:, tt * 40:(tt + 1) * 40],
                sb["rmn"][:, tt:tt + 1], None, ALU.is_equal))
        for tt in range(NTT):
            op("vector", lambda e, tt=tt: e.scalar_tensor_tensor(
                t40, sc[:, tt * 40:(tt + 1) * 40], SCALE,
                eq[:, tt * 40:(tt + 1) * 40], ALU.mult, ALU.mult))
            op("vector", lambda e, tt=tt: e.tensor_add(
                wfl[:, tt * 40:(tt + 1) * 40], wfl[:, tt * 40:(tt + 1) * 40], t40))
            op("vector", lambda e, tt=tt: e.scalar_tensor_tensor(
                sel[:, tt * 40:(tt + 1) * 40], eq[:, tt * 40:(tt + 1) * 40],
                -BIG2, sel[:, tt * 40:(tt + 1) * 40], ALU.mult, ALU.add))
    op("vector", lambda e: e.reduce_sum(
        sb["zwt"][:], wfl.rearrange("p (g x) -> p g x", g=NTT)[:, :, E_R:E_T], axis=AX.X))
    op("vector", lambda e: e.tensor_copy(
        wcat.rearrange("p (g x) -> p g x", g=NTT)[:, :, 0:40],
        wfl.rearrange("p (g x) -> p g x", g=NTT)))
    op("vector", lambda e: e.tensor_copy(
        wcat.rearrange("p (g x) -> p g x", g=NTT)[:, :, 40:41],
        sb["zwt"][:].unsqueeze(2)))
    v_wcat = ctx.at("dv")
    bank_free2 = 0
    for tt in range(NTT):
        w = [("dv", v_wcat), ("dv", v_ident)]
        if bank_free2:
            w.append(("ac", bank_free2))
        op("tensor", lambda e, tt=tt: e.transpose(
            ps[2][0:41, 0:128],
            wcat[:, tt * 41:(tt + 1) * 41], sb["ident"][:]), waits=w)
        v_tp = ctx.at("pe")
        op("scalar", lambda e, tt=tt: e.copy(
            sb["wfm"][:, tt * 128:(tt + 1) * 128], ps[2][0:41, 0:128]),
            waits=[("pe", v_tp)])
        bank_free2 = ctx.at("ac")
    ctx.force_small = False


def acc_init(ctx, sb, ps, v_ident, v_ohb):
    op = ctx.op
    A, B, C = sb["A"], sb["B"], sb["C"]
    C8 = C[:].rearrange("p (c t) -> p c t", c=8)
    bank_free = [0, 0]
    for tt in range(NTT):
        for half in range(2):
            bank = half
            for k in range(4):
                cchunk = half * 4 + k
                w = [("dv", v_ident)]
                if k == 0:
                    w.append(("dv", bank_free[bank]))
                op("tensor", lambda e, tt=tt, cchunk=cchunk, bank=bank, k=k: e.transpose(
                    ps[bank][:, k * 128:(k + 1) * 128],
                    A[:, tt * 1024 + cchunk * 128:tt * 1024 + (cchunk + 1) * 128],
                    sb["ident"][:]),
                    waits=w, inc=(k == 3))
            v_tp = ctx.at("pe")
            op("vector", lambda e, half=half, tt=tt, bank=bank: e.tensor_copy(
                C8[:, half * 4:half * 4 + 4, tt * 128:(tt + 1) * 128],
                ps[bank][:, 0:512].rearrange("p (c t) -> p c t", c=4)),
                waits=[("pe", v_tp)])
            bank_free[bank] = ctx.at("dv")
    v_accinit_tp = ctx.at("pe")
    v_accinit_cp = ctx.at("dv")
    op("vector", lambda e: e.tensor_scalar(
        sb["oh"][:], sb["ohb"][:], 40.0, None, ALU.is_equal),
        waits=[("gp", v_ohb)])
    v_oh = ctx.at("dv")
    ps4_free = v_accinit_cp
    for ts in range(4):
        v_mm = op("tensor", lambda e, ts=ts: e.matmul(
            ps[4][:, 0:512], sb["oh"][:],
            sb["wfm"][0:41, ts * 512:(ts + 1) * 512],
            start=True, stop=True),
            waits=[("dv", v_oh), ("dv", ps4_free)])
        for c in range(8):
            op("vector", lambda e, c=c, ts=ts: e.tensor_tensor(
                sb["stg"][:], sb["B"][:, c * 2048 + ts * 512:c * 2048 + (ts + 1) * 512],
                ps[4][:, 0:512], ALU.mult),
                waits=[("pe", v_mm)], small=True)
            op("vector", lambda e, c=c, ts=ts: e.tensor_tensor(
                C[:, c * 2048 + ts * 512:c * 2048 + (ts + 1) * 512],
                C[:, c * 2048 + ts * 512:c * 2048 + (ts + 1) * 512],
                sb["stg"][:], ALU.add), small=True)
        ps4_free = ctx.at("dv")
    return v_accinit_tp


def moe_experts(ctx, sb, ps, dr, v_gu_ld, v_dn_ld):
    op = ctx.op
    A, B, C = sb["A"], sb["B"], sb["C"]
    TS = 256
    NTS = T // TS
    NSLOT = E_R * NTS
    Bhi3 = B[:, 0:8192].bitcast(BF16).rearrange("p (c t) -> p c t", c=8)
    Blo3 = B[:, 8192:16384].bitcast(BF16).rearrange("p (c t) -> p c t", c=8)
    whi = A[:, 0:4096].bitcast(BF16)
    wlo = A[:, 4096:8192].bitcast(BF16)
    dhi = A[:, 8192:10240].bitcast(BF16)
    dlo = A[:, 10240:12288].bitcast(BF16)
    sus = A[:, A_SUS:A_SUS + 1024]
    v_dv0 = ctx.at("dv")
    st = {
        "gu_rdy": v_gu_ld, "dn_rdy": v_dn_ld,
        "gu_bank_free": [("dv", v_dv0)] * 4,
        "dn_bank_free": [v_dv0] * 4,
        "su_free": [0, 0],
        "oh_used": ctx.at("pe"),
        "wbs_ready": [v_dv0, v_dv0],
        "oh_cur": -1,
        "stage_dma": None,
    }

    def build_wb(slot, p4_wait):
        e2, ts2 = slot // NTS, slot % NTS
        if e2 != st["oh_cur"]:
            st["oh_cur"] = e2
            op("vector", lambda ee, e2=e2: ee.tensor_scalar(
                sb["oh"][:], sb["ohb"][:], float(e2), None, ALU.is_equal),
                waits=[("pe", st["oh_used"])], small=True)
        v_oh = ctx.at("dv")
        v_mm = op("tensor", lambda ee, ts2=ts2: ee.matmul(
            ps[4][:, 0:TS], sb["oh"][:],
            sb["wfm"][0:41, ts2 * TS:(ts2 + 1) * TS],
            start=True, stop=True),
            waits=[("dv", v_oh), ("dv", p4_wait)])
        st["oh_used"] = ctx.at("pe")
        v_cp = op("vector", lambda ee, slot=slot: ee.tensor_copy(
            sb["wbs"][:, (slot % 2) * TS:(slot % 2) * TS + TS], ps[4][:, 0:TS]),
            waits=[("pe", v_mm)], small=True)
        st["wbs_ready"][slot % 2] = v_cp
        return v_cp

    def gate_up(e, ts):
        groups = []
        for oc in range(8):
            bank = oc // 2
            col0 = (oc % 2) * TS
            for kc in range(NCH):
                lh = whi[:, kc * 1024 + oc * 128:kc * 1024 + (oc + 1) * 128]
                ll = wlo[:, kc * 1024 + oc * 128:kc * 1024 + (oc + 1) * 128]
                rh = Bhi3[:, kc, ts * TS:(ts + 1) * TS]
                rl = Blo3[:, kc, ts * TS:(ts + 1) * TS]
                for pi, (lt, rt) in enumerate(((lh, rh), (lh, rl), (ll, rh))):
                    w = []
                    if oc % 2 == 0 and kc == 0 and pi == 0:
                        w.append(("gu", st["gu_rdy"]))
                        if st["gu_bank_free"][bank] is not None:
                            w.append(st["gu_bank_free"][bank])
                    last = (oc % 2 == 1 and kc == NCH - 1 and pi == 2)
                    op("tensor", lambda ee, lt=lt, rt=rt, bank=bank, col0=col0,
                       first=(oc % 2 == 0 and kc == 0 and pi == 0), last=last: ee.matmul(
                        ps[bank][:, col0:col0 + TS], lt, rt,
                        start=first, stop=last, skip_group_check=True),
                        waits=w, inc=last)
            if oc % 2 == 1:
                groups.append(ctx.at("pe"))
        return groups

    def su_evac(e, ts, v_groups):
        su0 = A_SU[ts % 2]
        v_silu = [0] * 4
        for ic in range(4):
            gbank, gcol = ic // 2, (ic % 2) * TS
            w = [("pe", v_groups[ic // 2])]
            if ic == 0:
                w.append(("pe", st["su_free"][ts % 2]))
                if "su_scr_free" in st:
                    w.append(("dv", st["su_scr_free"]))
            v_silu[ic] = op("scalar", lambda ee, ic=ic, gbank=gbank, gcol=gcol:
                            ee.activation(
                                sus[:, ic * TS:(ic + 1) * TS],
                                ps[gbank][:, gcol:gcol + TS], AF.Silu,
                                bias=0.0, scale=1.0),
                            waits=w)
        for half in range(2):
            ubank = 2 + half
            op("vector", lambda ee, half=half, ubank=ubank: ee.scalar_tensor_tensor(
                sus[:, half * 512:(half + 1) * 512],
                ps[ubank][:, 0:512], 1.0,
                sus[:, half * 512:(half + 1) * 512],
                ALU.mult, ALU.mult),
                waits=[("ac", v_silu[half * 2 + 1]), ("pe", v_groups[2 + half])])
        op("vector", lambda ee, ts=ts: ee.tensor_tensor(
            sus[:].rearrange("p (i t) -> p i t", i=4),
            sus[:].rearrange("p (i t) -> p i t", i=4),
            sb["wbs"][:, (ts % 2) * TS:(ts % 2) * TS + TS]
            .unsqueeze(1).broadcast_to([128, 4, TS]), ALU.mult),
            waits=[("dv", st["wbs_ready"][(e * NTS + ts) % 2])])
        suh = A[:, su0:su0 + 512].bitcast(BF16)
        sul = A[:, su0 + 512:su0 + 1024].bitcast(BF16)
        op("vector", lambda ee: ee.tensor_copy(suh, sus[:]))
        op("vector", lambda ee: ee.tensor_tensor(sul, sus[:], suh, ALU.subtract))
        vd, va = ctx.at("dv"), ctx.at("ac")
        st["gu_bank_free"] = [("ac", v_silu[1]), ("ac", v_silu[3]), ("dv", vd), ("dv", vd)]
        st["su_scr_free"] = vd
        return vd

    def down(e, ts, v_su):
        su0 = A_SU[ts % 2]
        suh = A[:, su0:su0 + 512].bitcast(BF16)
        sul = A[:, su0 + 512:su0 + 1024].bitcast(BF16)
        for oc in range(8):
            bank = 4 + oc // 2
            col0 = (oc % 2) * TS
            for ic in range(4):
                lh = dhi[:, ic * 1024 + oc * 128:ic * 1024 + (oc + 1) * 128]
                ll = dlo[:, ic * 1024 + oc * 128:ic * 1024 + (oc + 1) * 128]
                rh = suh[:, ic * TS:(ic + 1) * TS]
                rl = sul[:, ic * TS:(ic + 1) * TS]
                for pi, (lt, rt) in enumerate(((lh, rh), (lh, rl), (ll, rh))):
                    w = []
                    if oc % 2 == 0 and ic == 0 and pi == 0:
                        w.append(("dn", st["dn_rdy"]))
                        w.append(("dv", v_su))
                        w.append(("dv", st["dn_bank_free"][bank - 4]))
                    last = (oc % 2 == 1 and ic == 3 and pi == 2)
                    op("tensor", lambda ee, lt=lt, rt=rt, bank=bank, col0=col0,
                       first=(oc % 2 == 0 and ic == 0 and pi == 0), last=last: ee.matmul(
                        ps[bank][:, col0:col0 + TS], lt, rt,
                        start=first, stop=last, skip_group_check=True),
                        waits=w, inc=last)
        v_mm = ctx.at("pe")
        st["su_free"][ts % 2] = v_mm
        v_evac_p4 = None
        for oc in range(8):
            bank = 4 + oc // 2
            col0 = (oc % 2) * TS
            op("vector", lambda ee, oc=oc, ts=ts, bank=bank, col0=col0: ee.tensor_tensor(
                C[:, oc * 2048 + ts * TS:oc * 2048 + (ts + 1) * TS],
                C[:, oc * 2048 + ts * TS:oc * 2048 + (ts + 1) * TS],
                ps[bank][:, col0:col0 + TS], ALU.add),
                waits=[("pe", v_mm)])
            if oc == 1:
                v_evac_p4 = ctx.at("dv")
        v = ctx.at("dv")
        st["dn_bank_free"] = [v, v, v, v]
        return v_evac_p4

    v_cp0 = build_wb(0, v_dv0)
    v_cp1 = build_wb(1, v_cp0)
    st["dn_bank_free"][0] = v_cp1
    pend_down = None
    v_gu_end_prev = 0
    v_dn_end_prev = 0
    for e in range(E_R):
        for ts in range(NTS):
            slot = e * NTS + ts
            v_evac_p4 = None
            if pend_down is not None:
                pd_e, pd_ts, pd_su = pend_down
                v_evac_p4 = down(pd_e, pd_ts, pd_su)
                if pd_ts == NTS - 1:
                    v_dn_end_prev = ctx.at("pe")
            if pend_down is not None and pend_down[1] == NTS - 1 and pend_down[0] + 1 < E_R:
                st["dn_rdy"] = op("sync", lambda ee, e=e: ee.dma_start(
                    out=A[:, 8192:12288].bitcast(BF16), in_=dr["dn_hl"][e]),
                    waits=[("pe", v_dn_end_prev)], dma=True, sem="dn")
            v_groups = gate_up(e, ts)
            v_gu = v_groups[-1]
            if e + 1 < E_R and ts == NTS - 1:
                st["gu_rdy"] = op("sync", lambda ee, e=e: ee.dma_start(
                    out=A[:, 0:8192].bitcast(BF16), in_=dr["gu_hl"][e + 1]),
                    waits=[("pe", v_gu)], dma=True, sem="gu")
            v_su = su_evac(e, ts, v_groups)
            if slot + 2 < NSLOT:
                v_wbcp = build_wb(slot + 2,
                                  v_evac_p4 if v_evac_p4 is not None else st["dn_bank_free"][0])
                st["dn_bank_free"][0] = v_wbcp
            pend_down = (e, ts, v_su)
    down(*pend_down)


def out_phase(ctx, sb, ps, dr, v_ident, n_tiles=NTT):
    op = ctx.op
    A, C = sb["A"], sb["C"]
    NOB = 4
    ost_free = [0] * NOB
    bank_free = [0, 0]
    for tt in range(n_tiles):
        buf = tt % NOB
        o0 = buf * 1024
        for half in range(2):
            bank = half
            for k in range(4):
                cchunk = half * 4 + k
                w = [("dv", v_ident)]
                if k == 0:
                    w.append(("dv", bank_free[bank]))
                op("tensor", lambda e, tt=tt, cchunk=cchunk, bank=bank, k=k: e.transpose(
                    ps[bank][:, k * 128:(k + 1) * 128],
                    C[:, cchunk * 2048 + tt * 128:cchunk * 2048 + (tt + 1) * 128],
                    sb["ident"][:]),
                    waits=w, inc=(k == 3))
            v_tp = ctx.at("pe")
            op("vector", lambda e, half=half, bank=bank, o0=o0: e.tensor_copy(
                A[:, o0 + half * 512:o0 + (half + 1) * 512], ps[bank][:, 0:512]),
                waits=[("pe", v_tp), ("gp", ost_free[buf])])
            bank_free[bank] = ctx.at("dv")
        v_cp = ctx.at("dv")
        sname = "st0" if tt % 2 == 0 else "st1"
        v_st = op("gpsimd", lambda e, tt=tt, o0=o0: e.dma_start(
            out=dr["out"][tt * 128:(tt + 1) * 128, :], in_=A[:, o0:o0 + 1024]),
            waits=[("dv", v_cp)], dma=True, sem=sname)
        ost_free[buf] = op("gpsimd", lambda e: e.engine_nop(),
                           waits=[(sname, v_st)])


def stage_inputs(L0_attn_qkv, L0_attn_wo, L0_ffn_classifier_weight,
                 L0_ffn_e_score_correction_bias, L0_ffn_gate_up_proj,
                 L0_ffn_down_proj):
    f32 = np.float32
    qkvT = np.ascontiguousarray(np.asarray(L0_attn_qkv, dtype=f32).T)
    q5 = qkvT.reshape(8, 128, 3, 8, 128)
    qkv_st = np.ascontiguousarray(q5.transpose(3, 1, 0, 2, 4)).reshape(NHP, 128, 3072)
    woT = np.ascontiguousarray(np.asarray(L0_attn_wo, dtype=f32).T)
    wo_st = np.ascontiguousarray(
        woT.reshape(8, 128, 8, 128).transpose(1, 0, 2, 3)).reshape(128, 8192)
    wcT = np.ascontiguousarray(np.asarray(L0_ffn_classifier_weight, dtype=f32).T)
    wc_st = np.ascontiguousarray(wcT.reshape(8, 128, E_T).transpose(1, 0, 2)).reshape(128, 320)
    bias_st = np.ascontiguousarray(
        np.broadcast_to(np.asarray(L0_ffn_e_score_correction_bias, dtype=f32)[None, :],
                        (128, E_T)))
    gu = np.asarray(L0_ffn_gate_up_proj, dtype=f32)[:E_R]
    guT = gu.transpose(0, 2, 1)
    gu_st = np.ascontiguousarray(
        guT.reshape(E_R, 8, 128, 1024).transpose(0, 2, 1, 3)).reshape(E_R, 128, 8192)
    dn = np.asarray(L0_ffn_down_proj, dtype=f32)
    dnT = dn.transpose(0, 2, 1)
    dn_st = np.ascontiguousarray(
        dnT.reshape(E_R, 4, 128, 1024).transpose(0, 2, 1, 3)).reshape(E_R, 128, 4096)
    mask_st = build_mask_host()
    return dict(qkv_st=qkv_st, wo_st=wo_st, wc_st=wc_st, bias_st=bias_st,
                gu_st=gu_st, dn_st=dn_st, mask_st=mask_st)


def kernel(L0_attn_x, L0_attn_qkv, L0_attn_wo, L0_ffn_classifier_weight,
           L0_ffn_e_score_correction_bias, L0_ffn_gate_up_proj, L0_ffn_down_proj):
    x = np.asarray(L0_attn_x, dtype=np.float32)
    weights = stage_inputs(L0_attn_qkv, L0_attn_wo, L0_ffn_classifier_weight,
                           L0_ffn_e_score_correction_bias, L0_ffn_gate_up_proj,
                           L0_ffn_down_proj)
    nc = build_program()
    in_maps = []
    for c in range(8):
        m = dict(weights)
        m["x"] = np.ascontiguousarray(x[c])
        in_maps.append(m)
    res = run_bass_kernel_spmd(nc, in_maps, core_ids=list(range(8)))
    out = np.stack([res.results[c]["out"] for c in range(8)], axis=0)
    return out.astype(np.float32)


# version 12
# speedup vs baseline: 1.1328x; 1.0209x over previous
import contextlib
import numpy as np
import ml_dtypes
import concourse.bass as bass
import concourse.mybir as mybir
from concourse.bass_utils import run_bass_kernel_spmd

F32 = mybir.dt.float32
BF16 = mybir.dt.bfloat16
AF = mybir.ActivationFunctionType
ALU = mybir.AluOpType
AX = mybir.AxisListType

T = 2048
DM = 1024
NH = 16
DH = 64
NHP = 8
NCH = 8
NTT = 16
E_R, E_T, TOPK = 32, 40, 6
SCALE = 2.5
EPS = 1e-5
NEG = -1e30
BIGI = 65536.0
BIG2 = 1e30
WIN, DIL = 128, 4

ENGINES = ("tensor", "vector", "scalar", "sync", "gpsimd")
ESEM = {"tensor": "pe", "vector": "dv", "scalar": "ac", "sync": "sy", "gpsimd": "gp"}

DEBUG = 0
STOP_AFTER = None

C_QW = 0
C_QF = 3072
C_KF = 5120
C_VT = 7168
C_PB = 9216
C_PTS = 11264
C_VS = 11776
C_MSK = 13312
C_XT0, C_XT1, C_LNT = 9216, 10240, 11264
C_WOW = 0
C_XL0, C_XL1 = 8192, 12288
C_LNT2 = 0
C_SC = 0
C_SEL = 640
C_WFL = 1280
C_EQ = 1920
C_MI = 2560
C_T40 = 3200
C_WCAT = 3264
A_WGU = 0
A_WD = 8192
A_SU = (12288, 13312)
A_SUS = 14336
A_OST = (0, 1024)
CV_STG = 14336

N_MASK_TILES = sum(min(3, q + 1) for q in range(NTT)) + 1
T4_COL = (N_MASK_TILES - 1) * 128


def _boundary_index(qi, jb):
    cum = sum(min(3, q + 1) for q in range(qi))
    return cum + (jb - max(0, qi - 2))


def build_mask_host():
    out = np.zeros((128, N_MASK_TILES * 128), dtype=np.float32)
    col = 0
    for qi in range(NTT):
        i = qi * 128 + np.arange(128)[:, None]
        for jb in range(max(0, qi - 2), qi + 1):
            j = jb * 128 + np.arange(128)[None, :]
            win = (j <= i) & (j >= i - WIN + 1)
            base = i - WIN - DIL
            dil = (j <= base) & ((base - j) % DIL == 0)
            out[:, col:col + 128] = np.where(win | dil, 0.0, NEG)
            col += 128
    p = np.arange(128)[:, None]
    jj = np.arange(128)[None, :]
    out[:, col:col + 128] = np.where((p - jj) % DIL == 0, 0.0, NEG)
    return out.astype(ml_dtypes.bfloat16)


class Ctx:

    def __init__(self, blk, sems, eng_name, eng):
        self.blk = blk
        self.sems = sems
        self.counts = {k: 0 for k in sems}
        self.eng_name = eng_name
        self.eng = eng
        self.pending_small = {}
        self.force_small = False

    def at(self, s):
        return self.counts[s]

    def op(self, engine, emit, waits=(), dma=False, sem=None, inc=True, small=False):
        if self.force_small and engine == "vector":
            small = True
        semname = (sem or ESEM[engine]) if inc else None
        amt = 16 if dma else 1
        tgt = None
        if semname is not None:
            self.counts[semname] += amt
            tgt = self.counts[semname]
        psmall = self.pending_small.get(engine)
        self.pending_small[engine] = (semname, tgt) if (small and semname) else None
        if self.eng_name == engine:
            if psmall is not None:
                self.eng.wait_ge(self.sems[psmall[0]], psmall[1])
            for s, v in waits:
                if v > 0:
                    self.eng.wait_ge(self.sems[s], v)
            instr = emit(self.eng)
            if semname is not None:
                instr.then_inc(self.sems[semname], amt)
        return tgt

    def barrier(self):
        snap = dict(self.counts)
        own = ESEM[self.eng_name]
        for s, v in snap.items():
            if v > 0 and s != own:
                self.eng.wait_ge(self.sems[s], v)

    def final_wait(self, engine, semname, value):
        if self.eng_name == engine:
            self.eng.wait_ge(self.sems[semname], value)


def build_program():
    nc = bass.Bass("TRN2", target_bir_lowering=False)
    dr = {}
    dr["x"] = nc.dram_tensor("x", [T, DM], F32, kind="ExternalInput").ap()
    dr["qkv_st"] = nc.dram_tensor("qkv_st", [NHP, 128, 3072], F32, kind="ExternalInput").ap()
    dr["wo_st"] = nc.dram_tensor("wo_st", [128, 8192], F32, kind="ExternalInput").ap()
    dr["wc_st"] = nc.dram_tensor("wc_st", [128, 320], F32, kind="ExternalInput").ap()
    dr["bias_st"] = nc.dram_tensor("bias_st", [128, E_T], F32, kind="ExternalInput").ap()
    dr["gu_st"] = nc.dram_tensor("gu_st", [E_R, 128, 8192], F32, kind="ExternalInput").ap()
    dr["dn_st"] = nc.dram_tensor("dn_st", [E_R, 128, 4096], F32, kind="ExternalInput").ap()
    dr["mask_st"] = nc.dram_tensor("mask_st", [128, N_MASK_TILES * 128], BF16,
                                   kind="ExternalInput").ap()
    dr["out"] = nc.dram_tensor("out", [T, DM], F32, kind="ExternalOutput").ap()
    dr["gu_hl"] = nc.dram_tensor("gu_hl", [E_R, 128, 16384], BF16, kind="Internal").ap()
    dr["dn_hl"] = nc.dram_tensor("dn_hl", [E_R, 128, 8192], BF16, kind="Internal").ap()
    if DEBUG:
        for nm, shape in [("dbg_a", [128, 16384]), ("dbg_b", [128, 16384]),
                          ("dbg_c", [128, 16384]), ("dbg_wfm", [41, 2048]),
                          ("dbg_s", [128, 64])]:
            dr[nm] = nc.dram_tensor(nm, shape, F32, kind="ExternalOutput").ap()

    with contextlib.ExitStack() as est:
        sb = {}
        for nm, shape in [("A", [128, 16384]), ("B", [128, 16384]), ("C", [128, 16384]),
                          ("ident", [128, 128]), ("oh", [41, 128]), ("ohb", [41, 128]),
                          ("wcw", [128, 320]), ("bias40", [128, E_T]), ("iotaB", [128, E_T]),
                          ("wfm", [41, 2048]), ("stg", [128, 512]), ("stats", [128, 12]),
                          ("mv", [128, 2]), ("sig", [128, 1]), ("rinv", [128, 1]),
                          ("negmax", [128, 2]), ("ell", [128, 2]), ("linv", [128, 2]),
                          ("gneg", [128, NTT]), ("gsinv", [128, NTT]), ("zwt", [128, NTT]),
                          ("rmx", [128, NTT]), ("rmn", [128, NTT]),
                          ("wbs", [128, 512])] + ([("dbgs", [128, 64])] if DEBUG else []):
            sb[nm] = est.enter_context(nc.sbuf_tensor(nm, shape, F32))
        ps = [est.enter_context(nc.psum_tensor(f"ps{i}", [128, 512], F32)) for i in range(8)]
        semnames = ["pe", "dv", "ac", "sy", "gp",
                    "xs0", "xs1", "cq", "cst", "gu", "dn", "st0", "st1"]
        sems = {s: est.enter_context(nc.semaphore(s)) for s in semnames}
        blk = est.enter_context(nc.Block())

        for en in ENGINES:
            def section(eng, en=en):
                ctx = Ctx(blk, sems, en, eng)
                ctx.nc = nc
                emit_all(ctx, dr, sb, ps)
            getattr(blk, en)(section)
    return nc


def emit_all(ctx, dr, sb, ps):
    op = ctx.op
    A, B, C = sb["A"], sb["B"], sb["C"]
    ident = sb["ident"]
    masksv = C[:, C_MSK:C_MSK + N_MASK_TILES * 64].bitcast(BF16)
    A8 = A[:].rearrange("p (c t) -> p c t", c=8)
    B8 = B[:].rearrange("p (c t) -> p c t", c=8)
    C8 = C[:].rearrange("p (c t) -> p c t", c=8)

    v = op("gpsimd", lambda e: e.iota(ident[:], pattern=[[-1, 128]], base=0,
                                      channel_multiplier=1, allow_small_or_imprecise_dtypes=True))
    op("vector", lambda e: e.tensor_scalar(ident[:], ident[:], 0.0, None, ALU.is_equal),
       waits=[("gp", v)])
    v_ident = ctx.at("dv")
    v = op("gpsimd", lambda e: e.iota(sb["iotaB"][:], pattern=[[1, E_T]], base=0,
                                      channel_multiplier=0, allow_small_or_imprecise_dtypes=True))
    op("vector", lambda e: e.tensor_scalar_add(sb["iotaB"][:], sb["iotaB"][:], BIGI),
       waits=[("gp", v)])
    op("gpsimd", lambda e: e.iota(sb["ohb"][:], pattern=[[0, 128]], base=0,
                                  channel_multiplier=1, allow_small_or_imprecise_dtypes=True))
    v_ohb = ctx.at("gp")
    op("sync", lambda e: e.dma_start(out=sb["wcw"][:], in_=dr["wc_st"]), dma=True, sem="cst")
    op("sync", lambda e: e.dma_start(out=sb["bias40"][:], in_=dr["bias_st"]), dma=True, sem="cst")
    v_mask_ld = op("sync", lambda e: e.dma_start(out=masksv, in_=dr["mask_st"]), dma=True, sem="cst")
    v_qw_ld = op("sync", lambda e: e.dma_start(out=C[:, C_QW:C_QW + 3072],
                                               in_=dr["qkv_st"][0]), dma=True, sem="cq")

    def stopped(phase):
        if STOP_AFTER != phase:
            return False
        if DEBUG:
            ctx.barrier()
            for nm, src in [("dbg_a", sb["A"][:]), ("dbg_b", sb["B"][:]),
                            ("dbg_c", sb["C"][:]), ("dbg_wfm", sb["wfm"][:]),
                            ("dbg_s", sb["dbgs"][:])]:
                op("gpsimd", lambda e, nm=nm, src=src: e.dma_start(out=dr[nm], in_=src),
                   dma=True, sem="st0")
            ctx.final_wait("gpsimd", "st0", ctx.at("st0"))
        return True

    cvst = {"out": [0, 0], "q": 0}

    def convert_experts(es):
        C7 = sb["B"]
        for e in es:
            for q in range(12):
                par = cvst["q"] % 2
                cvst["q"] += 1
                base = CV_STG - 2048 + par * 2048
                stg_in = C7[:, base:base + 1024]
                stg_hi = C7[:, base + 1024:base + 1536].bitcast(BF16)
                stg_lo = C7[:, base + 1536:base + 2048].bitcast(BF16)
                if q < 8:
                    src = dr["gu_st"][e][:, q * 1024:(q + 1) * 1024]
                    dhi = dr["gu_hl"][e][:, q * 1024:(q + 1) * 1024]
                    dlo = dr["gu_hl"][e][:, 8192 + q * 1024:8192 + (q + 1) * 1024]
                else:
                    qq = q - 8
                    src = dr["dn_st"][e][:, qq * 1024:(qq + 1) * 1024]
                    dhi = dr["dn_hl"][e][:, qq * 1024:(qq + 1) * 1024]
                    dlo = dr["dn_hl"][e][:, 4096 + qq * 1024:4096 + (qq + 1) * 1024]
                v_in = op("sync", lambda ee, src=src, stg_in=stg_in: ee.dma_start(
                    out=stg_in, in_=src),
                    waits=[("dn", cvst["out"][par])], dma=True, sem="gu")
                op("gpsimd", lambda ee, stg_hi=stg_hi, stg_in=stg_in: ee.tensor_copy(
                    stg_hi, stg_in),
                   waits=[("gu", v_in), ("dn", cvst["out"][par])])
                v_sp = op("gpsimd", lambda ee, stg_lo=stg_lo, stg_in=stg_in,
                          stg_hi=stg_hi: ee.tensor_tensor(
                    stg_lo, stg_in, stg_hi, ALU.subtract))
                op("sync", lambda ee, dhi=dhi, stg_hi=stg_hi: ee.dma_start(
                    out=dhi, in_=stg_hi), waits=[("gp", v_sp)], dma=True, sem="dn")
                cvst["out"][par] = op("sync", lambda ee, dlo=dlo, stg_lo=stg_lo: ee.dma_start(
                    out=dlo, in_=stg_lo), dma=True, sem="dn")

    with ctx.nc.named_scope("p1_ln1"):
        ln_to_fm(ctx, sb, ps, dst8=A8, src_kind="hbm", dr=dr, v_ident=v_ident)
    convert_experts(range(0, 8))
    ctx.barrier()
    if stopped("ln1"):
        return

    st = {"sbank_free": [None] * 4,
          "p4_free": 0,
          "obank_free": [0, 0],
          "pts_free": 0,
          "p_free": [0, 0],
          "exp_done": [0, 0],
          "qkv_bank_free": [None] * 4,
          "vs_free": 0, "vt_free": 0,
          "qw_ld": v_qw_ld}
    with ctx.nc.named_scope("p2_attn"):
        for hp in range(NHP):
            st["conv_dn"] = max(cvst["out"])
            attn_headpair(ctx, sb, ps, dr, hp, st, v_ident, v_mask_ld)
            if hp < 4:
                convert_experts(range(8 + hp * 6, 8 + (hp + 1) * 6))
    ctx.barrier()
    if stopped("attn"):
        return

    v_wo_ld = op("sync", lambda e: e.dma_start(out=C[:, C_WOW:C_WOW + 8192],
                                               in_=dr["wo_st"]), dma=True, sem="cst")
    with ctx.nc.named_scope("p3_wo"):
        wo_phase(ctx, sb, ps, dr, v_wo_ld, v_ident)
    ctx.barrier()
    if stopped("wo"):
        return

    with ctx.nc.named_scope("p4_ln2"):
        ln_to_fm(ctx, sb, ps, dst8=B8, src_kind="A", dr=dr, v_ident=v_ident)
    ctx.barrier()
    if stopped("ln2"):
        return

    with ctx.nc.named_scope("p5_route"):
        routing(ctx, sb, ps, v_ident)
    ctx.barrier()
    if stopped("route"):
        return

    with ctx.nc.named_scope("p6_init"):
        v_accinit_tp = acc_init(ctx, sb, ps, v_ident, v_ohb)
    Bhi_full = B[:, 0:8192].bitcast(BF16)
    Blo_full = B[:, 8192:16384].bitcast(BF16)
    op("vector", lambda e: e.tensor_copy(A[:], B[:]), waits=[("pe", v_accinit_tp)])
    op("vector", lambda e: e.tensor_copy(Bhi_full, A[:]))
    v_conv = op("vector", lambda e: e.tensor_tensor(Blo_full, A[:], Bhi_full, ALU.subtract))
    v_gu_ld = op("sync", lambda e: e.dma_start(
        out=A[:, 0:8192].bitcast(BF16), in_=dr["gu_hl"][0]),
        waits=[("pe", v_accinit_tp), ("dv", v_conv)], dma=True, sem="gu")
    v_dn_ld = op("sync", lambda e: e.dma_start(
        out=A[:, 8192:12288].bitcast(BF16), in_=dr["dn_hl"][0]),
        waits=[("pe", v_accinit_tp)], dma=True, sem="dn")
    ctx.barrier()
    if stopped("init"):
        return

    with ctx.nc.named_scope("p7_moe"):
        moe_experts(ctx, sb, ps, dr, v_gu_ld, v_dn_ld)
    ctx.barrier()
    if stopped("moe"):
        return

    if DEBUG:
        vs = []
        vs.append(op("gpsimd", lambda e: e.dma_start(out=dr["dbg_a"], in_=A[:]), dma=True, sem="st0"))
        vs.append(op("gpsimd", lambda e: e.dma_start(out=dr["dbg_b"], in_=B[:]), dma=True, sem="st0"))
        vs.append(op("gpsimd", lambda e: e.dma_start(out=dr["dbg_c"], in_=C[:]), dma=True, sem="st0"))
        vs.append(op("gpsimd", lambda e: e.dma_start(out=dr["dbg_wfm"], in_=sb["wfm"][:]),
                     dma=True, sem="st0"))

    n_tiles = int(STOP_AFTER[3:]) if (STOP_AFTER or "").startswith("out") else NTT
    with ctx.nc.named_scope("p8_out"):
        out_phase(ctx, sb, ps, dr, v_ident, n_tiles)
    ctx.final_wait("gpsimd", "st0", ctx.at("st0"))
    ctx.final_wait("gpsimd", "st1", ctx.at("st1"))


def ln_to_fm(ctx, sb, ps, dst8, src_kind, dr, v_ident):
    op = ctx.op
    C = sb["C"]
    A = sb["A"]
    last_lnt_reader = 0
    last_xt_reader = [0, 0]
    bank_free = [0, 0]
    lnt_off = C_LNT if src_kind == "hbm" else C_LNT2
    for tt in range(NTT):
        buf = tt % 2
        if src_kind == "hbm":
            xt_off = C_XT0 if buf == 0 else C_XT1
            xsem = "xs0" if buf == 0 else "xs1"
            v_x = op("gpsimd", lambda e, tt=tt, xt_off=xt_off: e.dma_start(
                out=C[:, xt_off:xt_off + 1024],
                in_=dr["x"][tt * 128:(tt + 1) * 128, :]),
                waits=[("dv", last_xt_reader[buf])], dma=True, sem=xsem)
            src = C[:, xt_off:xt_off + 1024]
            w0 = [(xsem, v_x)]
        else:
            src = A[:, tt * 1024:(tt + 1) * 1024]
            w0 = []
        v_red = op("vector", lambda e, src=src: e.reduce_sum(sb["mv"][:, 0:1], src, axis=AX.X),
                   waits=w0, small=True)
        v_sq = op("scalar", lambda e, src=src, lnt_off=lnt_off: e.activation(
            C[:, lnt_off:lnt_off + 1024], src, AF.Square, bias=0.0, scale=1.0,
            accum_out=sb["mv"][:, 1:2]),
            waits=w0 + [("pe", last_lnt_reader), ("dv", v_red)])
        op("vector", lambda e: e.tensor_scalar_mul(sb["mv"][:, 0:1], sb["mv"][:, 0:1], 1.0 / 1024.0),
           waits=[("ac", v_sq)], small=True)
        op("vector", lambda e: e.tensor_tensor(sb["stats"][:, 0:1], sb["mv"][:, 0:1],
                                               sb["mv"][:, 0:1], ALU.mult), small=True)
        op("vector", lambda e: e.tensor_scalar(sb["sig"][:], sb["mv"][:, 1:2], 1.0 / 1024.0,
                                               sb["stats"][:, 0:1], ALU.mult, ALU.subtract), small=True)
        v_sig = op("vector", lambda e: e.tensor_scalar_add(sb["sig"][:], sb["sig"][:], EPS),
                   small=True)
        v_sqrt = op("scalar", lambda e: e.activation(sb["sig"][:], sb["sig"][:], AF.Sqrt,
                                                     bias=0.0, scale=1.0),
                    waits=[("dv", v_sig)])
        op("vector", lambda e: e.reciprocal(sb["rinv"][:], sb["sig"][:]), waits=[("ac", v_sqrt)],
           small=True)
        if DEBUG and src_kind == "hbm":
            op("vector", lambda e, tt=tt: e.tensor_copy(sb["dbgs"][:, tt*4:tt*4+2], sb["mv"][:]))
            op("vector", lambda e, tt=tt: e.tensor_copy(sb["dbgs"][:, tt*4+2:tt*4+3], sb["sig"][:]))
            op("vector", lambda e, tt=tt: e.tensor_copy(sb["dbgs"][:, tt*4+3:tt*4+4], sb["rinv"][:]))
        v_ln = op("vector", lambda e, src=src, lnt_off=lnt_off: e.tensor_scalar(
            C[:, lnt_off:lnt_off + 1024], src,
            sb["mv"][:, 0:1], sb["rinv"][:], ALU.subtract, ALU.mult),
            waits=[("pe", last_lnt_reader)])
        if src_kind == "hbm":
            last_xt_reader[buf] = v_ln
        for half in range(2):
            bank = half
            for k in range(4):
                cchunk = half * 4 + k
                w = [("dv", v_ln), ("dv", v_ident)]
                if k == 0:
                    w.append(("ac", bank_free[bank]))
                op("tensor", lambda e, cchunk=cchunk, bank=bank, k=k, lnt_off=lnt_off: e.transpose(
                    ps[bank][:, k * 128:(k + 1) * 128],
                    C[:, lnt_off + cchunk * 128:lnt_off + (cchunk + 1) * 128],
                    sb["ident"][:]),
                    waits=w, inc=(k == 3))
            v_tp = ctx.at("pe")
            last_lnt_reader = v_tp
            op("scalar", lambda e, half=half, tt=tt, bank=bank: e.copy(
                dst8[:, half * 4:half * 4 + 4, tt * 128:(tt + 1) * 128],
                ps[bank][:, 0:512].rearrange("p (c t) -> p c t", c=4)),
                waits=[("pe", v_tp)])
            bank_free[bank] = ctx.at("ac")


def attn_headpair(ctx, sb, ps, dr, hp, st, v_ident, v_mask_ld):
    op = ctx.op
    A, C = sb["A"], sb["C"]
    A8 = A[:].rearrange("p (c t) -> p c t", c=8)

    bank_rot = 0
    for m in range(3):
        for ts in range(4):
            bank = bank_rot % 4
            bank_rot += 1
            for kc in range(NCH):
                w = [("cq", st["qw_ld"])]
                if kc == 0 and st["qkv_bank_free"][bank] is not None:
                    w.append(st["qkv_bank_free"][bank])
                op("tensor", lambda e, m=m, ts=ts, kc=kc, bank=bank: e.matmul(
                    ps[bank][:, 0:512],
                    C[:, C_QW + (kc * 3 + m) * 128:C_QW + (kc * 3 + m) * 128 + 128],
                    A8[:, kc, ts * 512:(ts + 1) * 512],
                    start=(kc == 0), stop=(kc == NCH - 1)),
                    waits=w, inc=(kc == NCH - 1))
            v_mm = ctx.at("pe")
            if m < 2:
                dst_off = C_QF if m == 0 else C_KF
                op("vector", lambda e, dst_off=dst_off, ts=ts, bank=bank: e.tensor_copy(
                    C[:, dst_off + ts * 512:dst_off + (ts + 1) * 512], ps[bank][:, 0:512]),
                    waits=[("pe", v_mm)])
                st["qkv_bank_free"][bank] = ("dv", ctx.at("dv"))
            else:
                v_vs = op("scalar", lambda e, bank=bank: e.copy(
                    C[:, C_VS:C_VS + 512], ps[bank][:, 0:512]),
                    waits=[("pe", v_mm), ("pe", st["vs_free"])])
                st["qkv_bank_free"][bank] = ("ac", v_vs)
                for k in range(4):
                    w = [("ac", v_vs), ("dv", v_ident)]
                    if k == 0:
                        w.append(("ac", st["p4_free"]))
                        w.append(("dv", st["vt_free"]))
                    op("tensor", lambda e, k=k: e.transpose(
                        ps[4][:, k * 128:(k + 1) * 128],
                        C[:, C_VS + k * 128:C_VS + (k + 1) * 128], sb["ident"][:]),
                        waits=w, inc=(k == 3))
                v_tp = ctx.at("pe")
                st["vs_free"] = v_tp
                op("vector", lambda e, ts=ts: e.tensor_copy(
                    C[:, C_VT + ts * 512:C_VT + (ts + 1) * 512], ps[4][:, 0:512]),
                    waits=[("pe", v_tp)])
                st["vt_free"] = ctx.at("dv")
    v_qkv_done = ctx.at("pe")
    for b in range(4):
        st["sbank_free"][b] = st["qkv_bank_free"][b]
    if hp + 1 < NHP:
        st["qw_ld"] = op("sync", lambda e, hp=hp: e.dma_start(
            out=C[:, C_QW:C_QW + 3072], in_=dr["qkv_st"][hp + 1]),
            waits=[("pe", v_qkv_done)], dma=True, sem="cq")

    def p_ap(par, lo, hi):
        if par == 0:
            return C[:, C_PB + lo:C_PB + hi]
        if hp == NHP - 1:
            return C[:, C_QW + lo:C_QW + hi]
        return sb["B"][:, (hp + 1) * 2048 + lo:(hp + 1) * 2048 + hi]

    def emit_ptpv(pend):
        head, qi, par, v_norm = pend
        h64 = head * 64
        jn = (qi + 1) * 128
        njc = (jn + 511) // 512
        first = True
        for jc in range(njc):
            w = min(512, jn - jc * 512)
            nk = w // 128
            for k in range(nk):
                wts = [("dv", v_norm), ("dv", v_ident)]
                if k == 0:
                    wts.append(("ac", st["p4_free"]))
                op("tensor", lambda e, jc=jc, k=k, par=par: e.transpose(
                    ps[4][:, k * 128:(k + 1) * 128],
                    p_ap(par, jc * 512 + k * 128, jc * 512 + (k + 1) * 128),
                    sb["ident"][:]),
                    waits=wts, inc=(k == nk - 1))
            v_pt = ctx.at("pe")
            st["p_free"][par] = v_pt
            v_ptc = op("scalar", lambda e, w=w: e.copy(
                C[:, C_PTS:C_PTS + w], ps[4][:, 0:w]),
                waits=[("pe", v_pt), ("pe", st["pts_free"])])
            st["p4_free"] = v_ptc
            for k in range(nk):
                wts = [("ac", v_ptc)]
                if first:
                    wts.append(("ac", st["obank_free"][head]))
                last = (jc == njc - 1 and k == nk - 1)
                op("tensor", lambda e, jc=jc, k=k, h64=h64, head=head, first=first,
                   last=last: e.matmul(
                    ps[5 + head][h64:h64 + 64, 0:128],
                    C[:, C_VT + (jc * 4 + k) * 128 + h64:C_VT + (jc * 4 + k) * 128 + h64 + 64],
                    C[:, C_PTS + k * 128:C_PTS + (k + 1) * 128],
                    start=first, stop=last,
                    tile_position=(0, h64)),
                    waits=wts, inc=last)
                first = False
        v_pv = ctx.at("pe")
        st["pts_free"] = v_pv
        op("scalar", lambda e, h64=h64, qi=qi, head=head: e.copy(
            sb["B"][h64:h64 + 64, hp * 2048 + qi * 128:hp * 2048 + (qi + 1) * 128],
            ps[5 + head][h64:h64 + 64, 0:128]),
            waits=[("pe", v_pv)])
        st["obank_free"][head] = ctx.at("ac")

    pend = None
    for head in range(2):
        h64 = head * 64
        for qi in range(NTT):
            it = head * NTT + qi
            par = it % 2
            jn = (qi + 1) * 128
            njc = (jn + 511) // 512
            s_done = []
            for jc in range(njc):
                w = min(512, jn - jc * 512)
                wts = [("dv", st["vt_free"])]
                if st["sbank_free"][jc] is not None:
                    wts.append(st["sbank_free"][jc])
                op("tensor", lambda e, jc=jc, w=w, qi=qi, h64=h64: e.matmul(
                    ps[jc][:, 0:w],
                    C[h64:h64 + 64, C_QF + qi * 128:C_QF + (qi + 1) * 128],
                    C[h64:h64 + 64, C_KF + jc * 512:C_KF + jc * 512 + w],
                    start=True, stop=True),
                    waits=wts)
                s_done.append(ctx.at("pe"))
            for jc in range(njc):
                w = min(512, jn - jc * 512)
                nblocks = w // 128
                jb0 = jc * 4
                npure = min(max(qi - 2 - jb0, 0), nblocks)
                waits = [("pe", s_done[jc]), ("pe", st["p_free"][par]),
                         ("cst", v_mask_ld)]
                if hp >= NHP - 3 and par == 1:
                    waits.append(("dn", st.get("conv_dn", 0)))
                if npure > 0:
                    op("vector", lambda e, jc=jc, npure=npure, par=par: e.scalar_tensor_tensor(
                        p_ap(par, jc * 512, jc * 512 + npure * 128),
                        ps[jc][:, 0:npure * 128], 0.125,
                        masksv_t4(sb).unsqueeze(1).broadcast_to([128, npure, 128]),
                        ALU.mult, ALU.add),
                        waits=waits)
                    waits = []
                if npure < nblocks:
                    nb = nblocks - npure
                    i0 = _boundary_index(qi, jb0 + npure)
                    op("vector", lambda e, jc=jc, npure=npure, nb=nb, i0=i0, par=par:
                       e.scalar_tensor_tensor(
                        p_ap(par, jc * 512 + npure * 128, jc * 512 + (npure + nb) * 128),
                        ps[jc][:, npure * 128:(npure + nb) * 128], 0.125,
                        masksv_all(sb)[:, i0 * 128:(i0 + nb) * 128],
                        ALU.mult, ALU.add),
                        waits=waits)
                st["sbank_free"][jc] = ("dv", ctx.at("dv"))
            v_max = op("vector", lambda e, jn=jn, par=par: e.reduce_max(
                sb["negmax"][:, par:par + 1], p_ap(par, 0, jn), axis=AX.X, negate=True),
                waits=[("ac", st["exp_done"][par])], small=True)
            v_exp = op("scalar", lambda e, jn=jn, par=par: e.activation(
                p_ap(par, 0, jn), p_ap(par, 0, jn), AF.Exp,
                bias=sb["negmax"][:, par:par + 1], scale=1.0,
                accum_out=sb["ell"][:, par:par + 1]),
                waits=[("dv", v_max)])
            st["exp_done"][par] = v_exp
            op("vector", lambda e, par=par: e.reciprocal(
                sb["linv"][:, par:par + 1], sb["ell"][:, par:par + 1]),
               waits=[("ac", v_exp)], small=True)
            v_norm = op("vector", lambda e, jn=jn, par=par: e.tensor_scalar_mul(
                p_ap(par, 0, jn), p_ap(par, 0, jn), sb["linv"][:, par:par + 1]))
            if pend is not None:
                emit_ptpv(pend)
            pend = (head, qi, par, v_norm)
    emit_ptpv(pend)


def masksv_t4(sb):
    return sb["C"][:, C_MSK:C_MSK + N_MASK_TILES * 64].bitcast(BF16)[:, T4_COL:T4_COL + 128]


def masksv_all(sb):
    return sb["C"][:, C_MSK:C_MSK + N_MASK_TILES * 64].bitcast(BF16)


def wo_phase(ctx, sb, ps, dr, v_wo_ld, v_ident):
    op = ctx.op
    A, B, C = sb["A"], sb["B"], sb["C"]
    B8 = B[:].rearrange("p (c t) -> p c t", c=8)
    xl_reader = [0, 0]
    v_xl = [0, 0]
    mmbank_free = [0, 0]
    tpbank_free = [0, 0]
    stg_free = 0
    for ts in range(4):
        buf = ts % 2
        xoff = C_XL0 if buf == 0 else C_XL1
        xsem = "xs0" if buf == 0 else "xs1"
        v_xl[buf] = op("gpsimd", lambda e, ts=ts, xoff=xoff: e.dma_start(
            out=C[:, xoff:xoff + 4096].rearrange("p (a f) -> p a f", a=4),
            in_=dr["x"][ts * 512:(ts + 1) * 512, :].rearrange("(a p) f -> p a f", p=128)),
            waits=[("dv", xl_reader[buf])], dma=True, sem=xsem)
        for oc in range(NCH):
            bank = oc % 2
            for kc in range(NCH):
                w = [("cst", v_wo_ld)]
                if kc == 0:
                    w.append(("ac", mmbank_free[bank]))
                op("tensor", lambda e, oc=oc, kc=kc, ts=ts, bank=bank: e.matmul(
                    ps[bank][:, 0:512],
                    C[:, C_WOW + kc * 1024 + oc * 128:C_WOW + kc * 1024 + (oc + 1) * 128],
                    B8[:, kc, ts * 512:(ts + 1) * 512],
                    start=(kc == 0), stop=(kc == NCH - 1)),
                    waits=w, inc=(kc == NCH - 1))
            v_mm = ctx.at("pe")
            v_stg = op("scalar", lambda e, bank=bank: e.copy(
                sb["stg"][:], ps[bank][:, 0:512]),
                waits=[("pe", v_mm), ("pe", stg_free)])
            mmbank_free[bank] = v_stg
            tb = 2 + (oc % 2)
            for k in range(4):
                w = [("ac", v_stg), ("dv", v_ident)]
                if k == 0:
                    w.append(("dv", tpbank_free[tb - 2]))
                op("tensor", lambda e, k=k, tb=tb: e.transpose(
                    ps[tb][:, k * 128:(k + 1) * 128],
                    sb["stg"][:, k * 128:(k + 1) * 128], sb["ident"][:]),
                    waits=w, inc=(k == 3))
            v_tp = ctx.at("pe")
            stg_free = v_tp
            op("vector", lambda e, ts=ts, oc=oc, tb=tb, xoff=xoff: e.tensor_tensor(
                A[:].rearrange("p (a f) -> p a f", a=NTT)[:, ts * 4:ts * 4 + 4, oc * 128:(oc + 1) * 128],
                ps[tb][:, 0:512].rearrange("p (a f) -> p a f", a=4),
                C[:, xoff:xoff + 4096].rearrange("p (a f) -> p a f", a=4)[:, :, oc * 128:(oc + 1) * 128],
                ALU.add),
                waits=[("pe", v_tp), ("xs0" if buf == 0 else "xs1", v_xl[buf])])
            tpbank_free[tb - 2] = ctx.at("dv")
            xl_reader[buf] = ctx.at("dv")


def routing(ctx, sb, ps, v_ident):
    op = ctx.op
    ctx.force_small = True
    B, C = sb["B"], sb["C"]
    B8 = B[:].rearrange("p (c t) -> p c t", c=8)
    sc = C[:, C_SC:C_SC + 640]
    sel = C[:, C_SEL:C_SEL + 640]
    wfl = C[:, C_WFL:C_WFL + 640]
    eq = C[:, C_EQ:C_EQ + 640]
    mi = C[:, C_MI:C_MI + 640]
    t40 = C[:, C_T40:C_T40 + 40]
    wcat = C[:, C_WCAT:C_WCAT + 656]

    bank_free = [0, 0]
    for tt in range(NTT):
        bank = tt % 2
        for kc in range(NCH):
            w = []
            if kc == 0:
                w.append(("dv", bank_free[bank]))
            op("tensor", lambda e, tt=tt, kc=kc, bank=bank: e.matmul(
                ps[bank][:, 0:E_T],
                B8[:, kc, tt * 128:(tt + 1) * 128],
                sb["wcw"][:, kc * E_T:(kc + 1) * E_T],
                start=(kc == 0), stop=(kc == NCH - 1)),
                waits=w, inc=(kc == NCH - 1))
        v_mm = ctx.at("pe")
        op("vector", lambda e, tt=tt, bank=bank: e.tensor_copy(
            sc[:, tt * 40:(tt + 1) * 40], ps[bank][:, 0:E_T]),
            waits=[("pe", v_mm)])
        bank_free[bank] = ctx.at("dv")
    sc3 = sc.rearrange("p (g x) -> p g x", g=NTT)
    op("vector", lambda e: e.reduce_max(sb["gneg"][:], sc3, axis=AX.X, negate=True))
    v_gm = ctx.at("dv")
    for tt in range(NTT):
        op("scalar", lambda e, tt=tt: e.activation(
            sc[:, tt * 40:(tt + 1) * 40], sc[:, tt * 40:(tt + 1) * 40], AF.Exp,
            bias=sb["gneg"][:, tt:tt + 1], scale=1.0),
            waits=[("dv", v_gm)])
    v_exp = ctx.at("ac")
    op("vector", lambda e: e.reduce_sum(sb["gsinv"][:], sc3, axis=AX.X), waits=[("ac", v_exp)])
    op("vector", lambda e: e.reciprocal(sb["gsinv"][:], sb["gsinv"][:]))
    for tt in range(NTT):
        op("vector", lambda e, tt=tt: e.tensor_scalar_mul(
            sc[:, tt * 40:(tt + 1) * 40], sc[:, tt * 40:(tt + 1) * 40],
            sb["gsinv"][:, tt:tt + 1]))
    for tt in range(NTT):
        op("vector", lambda e, tt=tt: e.tensor_add(
            sel[:, tt * 40:(tt + 1) * 40], sc[:, tt * 40:(tt + 1) * 40], sb["bias40"][:]))
    op("vector", lambda e: e.memset(wfl, 0.0))
    for _k in range(TOPK):
        op("vector", lambda e: e.reduce_max(sb["rmx"][:],
                                            sel.rearrange("p (g x) -> p g x", g=NTT), axis=AX.X))
        for tt in range(NTT):
            op("vector", lambda e, tt=tt: e.tensor_scalar(
                eq[:, tt * 40:(tt + 1) * 40], sel[:, tt * 40:(tt + 1) * 40],
                sb["rmx"][:, tt:tt + 1], None, ALU.is_equal))
        for tt in range(NTT):
            op("vector", lambda e, tt=tt: e.scalar_tensor_tensor(
                mi[:, tt * 40:(tt + 1) * 40], eq[:, tt * 40:(tt + 1) * 40],
                -BIGI, sb["iotaB"][:], ALU.mult, ALU.add))
        op("vector", lambda e: e.tensor_reduce(sb["rmn"][:],
                                               mi.rearrange("p (g x) -> p g x", g=NTT),
                                               axis=AX.X, op=ALU.min))
        for tt in range(NTT):
            op("vector", lambda e, tt=tt: e.tensor_scalar(
                eq[:, tt * 40:(tt + 1) * 40], mi[:, tt * 40:(tt + 1) * 40],
                sb["rmn"][:, tt:tt + 1], None, ALU.is_equal))
        for tt in range(NTT):
            op("vector", lambda e, tt=tt: e.scalar_tensor_tensor(
                t40, sc[:, tt * 40:(tt + 1) * 40], SCALE,
                eq[:, tt * 40:(tt + 1) * 40], ALU.mult, ALU.mult))
            op("vector", lambda e, tt=tt: e.tensor_add(
                wfl[:, tt * 40:(tt + 1) * 40], wfl[:, tt * 40:(tt + 1) * 40], t40))
            op("vector", lambda e, tt=tt: e.scalar_tensor_tensor(
                sel[:, tt * 40:(tt + 1) * 40], eq[:, tt * 40:(tt + 1) * 40],
                -BIG2, sel[:, tt * 40:(tt + 1) * 40], ALU.mult, ALU.add))
    op("vector", lambda e: e.reduce_sum(
        sb["zwt"][:], wfl.rearrange("p (g x) -> p g x", g=NTT)[:, :, E_R:E_T], axis=AX.X))
    op("vector", lambda e: e.tensor_copy(
        wcat.rearrange("p (g x) -> p g x", g=NTT)[:, :, 0:40],
        wfl.rearrange("p (g x) -> p g x", g=NTT)))
    op("vector", lambda e: e.tensor_copy(
        wcat.rearrange("p (g x) -> p g x", g=NTT)[:, :, 40:41],
        sb["zwt"][:].unsqueeze(2)))
    v_wcat = ctx.at("dv")
    bank_free2 = 0
    for tt in range(NTT):
        w = [("dv", v_wcat), ("dv", v_ident)]
        if bank_free2:
            w.append(("ac", bank_free2))
        op("tensor", lambda e, tt=tt: e.transpose(
            ps[2][0:41, 0:128],
            wcat[:, tt * 41:(tt + 1) * 41], sb["ident"][:]), waits=w)
        v_tp = ctx.at("pe")
        op("scalar", lambda e, tt=tt: e.copy(
            sb["wfm"][:, tt * 128:(tt + 1) * 128], ps[2][0:41, 0:128]),
            waits=[("pe", v_tp)])
        bank_free2 = ctx.at("ac")
    ctx.force_small = False


def acc_init(ctx, sb, ps, v_ident, v_ohb):
    op = ctx.op
    A, B, C = sb["A"], sb["B"], sb["C"]
    C8 = C[:].rearrange("p (c t) -> p c t", c=8)
    bank_free = [0, 0]
    for tt in range(NTT):
        for half in range(2):
            bank = half
            for k in range(4):
                cchunk = half * 4 + k
                w = [("dv", v_ident)]
                if k == 0:
                    w.append(("dv", bank_free[bank]))
                op("tensor", lambda e, tt=tt, cchunk=cchunk, bank=bank, k=k: e.transpose(
                    ps[bank][:, k * 128:(k + 1) * 128],
                    A[:, tt * 1024 + cchunk * 128:tt * 1024 + (cchunk + 1) * 128],
                    sb["ident"][:]),
                    waits=w, inc=(k == 3))
            v_tp = ctx.at("pe")
            op("vector", lambda e, half=half, tt=tt, bank=bank: e.tensor_copy(
                C8[:, half * 4:half * 4 + 4, tt * 128:(tt + 1) * 128],
                ps[bank][:, 0:512].rearrange("p (c t) -> p c t", c=4)),
                waits=[("pe", v_tp)])
            bank_free[bank] = ctx.at("dv")
    v_accinit_tp = ctx.at("pe")
    v_accinit_cp = ctx.at("dv")
    op("vector", lambda e: e.tensor_scalar(
        sb["oh"][:], sb["ohb"][:], 40.0, None, ALU.is_equal),
        waits=[("gp", v_ohb)])
    v_oh = ctx.at("dv")
    ps4_free = v_accinit_cp
    for ts in range(4):
        v_mm = op("tensor", lambda e, ts=ts: e.matmul(
            ps[4][:, 0:512], sb["oh"][:],
            sb["wfm"][0:41, ts * 512:(ts + 1) * 512],
            start=True, stop=True),
            waits=[("dv", v_oh), ("dv", ps4_free)])
        for c in range(8):
            op("vector", lambda e, c=c, ts=ts: e.tensor_tensor(
                sb["stg"][:], sb["B"][:, c * 2048 + ts * 512:c * 2048 + (ts + 1) * 512],
                ps[4][:, 0:512], ALU.mult),
                waits=[("pe", v_mm)], small=True)
            op("vector", lambda e, c=c, ts=ts: e.tensor_tensor(
                C[:, c * 2048 + ts * 512:c * 2048 + (ts + 1) * 512],
                C[:, c * 2048 + ts * 512:c * 2048 + (ts + 1) * 512],
                sb["stg"][:], ALU.add), small=True)
        ps4_free = ctx.at("dv")
    return v_accinit_tp


def moe_experts(ctx, sb, ps, dr, v_gu_ld, v_dn_ld):
    op = ctx.op
    A, B, C = sb["A"], sb["B"], sb["C"]
    TS = 256
    NTS = T // TS
    NSLOT = E_R * NTS
    Bhi3 = B[:, 0:8192].bitcast(BF16).rearrange("p (c t) -> p c t", c=8)
    Blo3 = B[:, 8192:16384].bitcast(BF16).rearrange("p (c t) -> p c t", c=8)
    whi = A[:, 0:4096].bitcast(BF16)
    wlo = A[:, 4096:8192].bitcast(BF16)
    dhi = A[:, 8192:10240].bitcast(BF16)
    dlo = A[:, 10240:12288].bitcast(BF16)
    sus = A[:, A_SUS:A_SUS + 1024]
    v_dv0 = ctx.at("dv")
    st = {
        "gu_rdy": v_gu_ld, "dn_rdy": v_dn_ld,
        "gu_bank_free": [("dv", v_dv0)] * 4,
        "dn_bank_free": [v_dv0] * 4,
        "su_free": [0, 0],
        "oh_used": ctx.at("pe"),
        "wbs_ready": [v_dv0, v_dv0],
        "oh_cur": -1,
        "stage_dma": None,
    }

    def build_wb(slot, p4_wait):
        e2, ts2 = slot // NTS, slot % NTS
        if e2 != st["oh_cur"]:
            st["oh_cur"] = e2
            op("vector", lambda ee, e2=e2: ee.tensor_scalar(
                sb["oh"][:], sb["ohb"][:], float(e2), None, ALU.is_equal),
                waits=[("pe", st["oh_used"])], small=True)
        v_oh = ctx.at("dv")
        v_mm = op("tensor", lambda ee, ts2=ts2: ee.matmul(
            ps[4][:, 0:TS], sb["oh"][:],
            sb["wfm"][0:41, ts2 * TS:(ts2 + 1) * TS],
            start=True, stop=True),
            waits=[("dv", v_oh), ("dv", p4_wait)])
        st["oh_used"] = ctx.at("pe")
        v_cp = op("vector", lambda ee, slot=slot: ee.tensor_copy(
            sb["wbs"][:, (slot % 2) * TS:(slot % 2) * TS + TS], ps[4][:, 0:TS]),
            waits=[("pe", v_mm)], small=True)
        st["wbs_ready"][slot % 2] = v_cp
        return v_cp

    def gate_up(e, ts):
        groups = []
        for oc in range(8):
            bank = oc // 2
            col0 = (oc % 2) * TS
            for kc in range(NCH):
                lh = whi[:, kc * 1024 + oc * 128:kc * 1024 + (oc + 1) * 128]
                ll = wlo[:, kc * 1024 + oc * 128:kc * 1024 + (oc + 1) * 128]
                rh = Bhi3[:, kc, ts * TS:(ts + 1) * TS]
                rl = Blo3[:, kc, ts * TS:(ts + 1) * TS]
                for pi, (lt, rt) in enumerate(((lh, rh), (lh, rl), (ll, rh))):
                    w = []
                    if oc % 2 == 0 and kc == 0 and pi == 0:
                        w.append(("gu", st["gu_rdy"]))
                        if st["gu_bank_free"][bank] is not None:
                            w.append(st["gu_bank_free"][bank])
                    last = (oc % 2 == 1 and kc == NCH - 1 and pi == 2)
                    op("tensor", lambda ee, lt=lt, rt=rt, bank=bank, col0=col0,
                       first=(oc % 2 == 0 and kc == 0 and pi == 0), last=last: ee.matmul(
                        ps[bank][:, col0:col0 + TS], lt, rt,
                        start=first, stop=last, skip_group_check=True),
                        waits=w, inc=last)
            if oc % 2 == 1:
                groups.append(ctx.at("pe"))
        return groups

    def su_evac(e, ts, v_groups):
        su0 = A_SU[ts % 2]
        v_silu = [0] * 4
        for ic in range(4):
            gbank, gcol = ic // 2, (ic % 2) * TS
            w = [("pe", v_groups[ic // 2])]
            if ic == 0:
                w.append(("pe", st["su_free"][ts % 2]))
                if "su_scr_free" in st:
                    w.append(("dv", st["su_scr_free"]))
            v_silu[ic] = op("scalar", lambda ee, ic=ic, gbank=gbank, gcol=gcol:
                            ee.activation(
                                sus[:, ic * TS:(ic + 1) * TS],
                                ps[gbank][:, gcol:gcol + TS], AF.Silu,
                                bias=0.0, scale=1.0),
                            waits=w)
        for half in range(2):
            ubank = 2 + half
            op("vector", lambda ee, half=half, ubank=ubank: ee.scalar_tensor_tensor(
                sus[:, half * 512:(half + 1) * 512],
                ps[ubank][:, 0:512], 1.0,
                sus[:, half * 512:(half + 1) * 512],
                ALU.mult, ALU.mult),
                waits=[("ac", v_silu[half * 2 + 1]), ("pe", v_groups[2 + half])])
        op("vector", lambda ee, ts=ts: ee.tensor_tensor(
            sus[:].rearrange("p (i t) -> p i t", i=4),
            sus[:].rearrange("p (i t) -> p i t", i=4),
            sb["wbs"][:, (ts % 2) * TS:(ts % 2) * TS + TS]
            .unsqueeze(1).broadcast_to([128, 4, TS]), ALU.mult),
            waits=[("dv", st["wbs_ready"][(e * NTS + ts) % 2])])
        suh = A[:, su0:su0 + 512].bitcast(BF16)
        sul = A[:, su0 + 512:su0 + 1024].bitcast(BF16)
        op("vector", lambda ee: ee.tensor_copy(suh, sus[:]))
        op("vector", lambda ee: ee.tensor_tensor(sul, sus[:], suh, ALU.subtract))
        vd, va = ctx.at("dv"), ctx.at("ac")
        st["gu_bank_free"] = [("ac", v_silu[1]), ("ac", v_silu[3]), ("dv", vd), ("dv", vd)]
        st["su_scr_free"] = vd
        return vd

    def down(e, ts, v_su):
        su0 = A_SU[ts % 2]
        suh = A[:, su0:su0 + 512].bitcast(BF16)
        sul = A[:, su0 + 512:su0 + 1024].bitcast(BF16)
        for oc in range(8):
            bank = 4 + oc // 2
            col0 = (oc % 2) * TS
            for ic in range(4):
                lh = dhi[:, ic * 1024 + oc * 128:ic * 1024 + (oc + 1) * 128]
                ll = dlo[:, ic * 1024 + oc * 128:ic * 1024 + (oc + 1) * 128]
                rh = suh[:, ic * TS:(ic + 1) * TS]
                rl = sul[:, ic * TS:(ic + 1) * TS]
                for pi, (lt, rt) in enumerate(((lh, rh), (lh, rl), (ll, rh))):
                    w = []
                    if oc % 2 == 0 and ic == 0 and pi == 0:
                        w.append(("dn", st["dn_rdy"]))
                        w.append(("dv", v_su))
                        w.append(("dv", st["dn_bank_free"][bank - 4]))
                    last = (oc % 2 == 1 and ic == 3 and pi == 2)
                    op("tensor", lambda ee, lt=lt, rt=rt, bank=bank, col0=col0,
                       first=(oc % 2 == 0 and ic == 0 and pi == 0), last=last: ee.matmul(
                        ps[bank][:, col0:col0 + TS], lt, rt,
                        start=first, stop=last, skip_group_check=True),
                        waits=w, inc=last)
        v_mm = ctx.at("pe")
        st["su_free"][ts % 2] = v_mm
        v_evac_p4 = None
        for oc in range(8):
            bank = 4 + oc // 2
            col0 = (oc % 2) * TS
            op("vector", lambda ee, oc=oc, ts=ts, bank=bank, col0=col0: ee.tensor_tensor(
                C[:, oc * 2048 + ts * TS:oc * 2048 + (ts + 1) * TS],
                C[:, oc * 2048 + ts * TS:oc * 2048 + (ts + 1) * TS],
                ps[bank][:, col0:col0 + TS], ALU.add),
                waits=[("pe", v_mm)])
            if oc == 1:
                v_evac_p4 = ctx.at("dv")
        v = ctx.at("dv")
        st["dn_bank_free"] = [v, v, v, v]
        return v_evac_p4

    v_cp0 = build_wb(0, v_dv0)
    v_cp1 = build_wb(1, v_cp0)
    st["dn_bank_free"][0] = v_cp1
    pend_down = None
    v_gu_end_prev = 0
    v_dn_end_prev = 0
    for e in range(E_R):
        for ts in range(NTS):
            slot = e * NTS + ts
            v_evac_p4 = None
            if pend_down is not None:
                pd_e, pd_ts, pd_su = pend_down
                v_evac_p4 = down(pd_e, pd_ts, pd_su)
                if pd_ts == NTS - 1:
                    v_dn_end_prev = ctx.at("pe")
            if pend_down is not None and pend_down[1] == NTS - 1 and pend_down[0] + 1 < E_R:
                st["dn_rdy"] = op("sync", lambda ee, e=e: ee.dma_start(
                    out=A[:, 8192:12288].bitcast(BF16), in_=dr["dn_hl"][e]),
                    waits=[("pe", v_dn_end_prev)], dma=True, sem="dn")
            v_groups = gate_up(e, ts)
            v_gu = v_groups[-1]
            if e + 1 < E_R and ts == NTS - 1:
                st["gu_rdy"] = op("sync", lambda ee, e=e: ee.dma_start(
                    out=A[:, 0:8192].bitcast(BF16), in_=dr["gu_hl"][e + 1]),
                    waits=[("pe", v_gu)], dma=True, sem="gu")
            v_su = su_evac(e, ts, v_groups)
            if slot + 2 < NSLOT:
                v_wbcp = build_wb(slot + 2,
                                  v_evac_p4 if v_evac_p4 is not None else st["dn_bank_free"][0])
                st["dn_bank_free"][0] = v_wbcp
            pend_down = (e, ts, v_su)
    down(*pend_down)


def out_phase(ctx, sb, ps, dr, v_ident, n_tiles=NTT):
    op = ctx.op
    A, C = sb["A"], sb["C"]
    NOB = 4
    ost_free = [0] * NOB
    bank_free = [0, 0]
    for tt in range(n_tiles):
        buf = tt % NOB
        o0 = buf * 1024
        for half in range(2):
            bank = half
            for k in range(4):
                cchunk = half * 4 + k
                w = [("dv", v_ident)]
                if k == 0:
                    w.append(("dv", bank_free[bank]))
                op("tensor", lambda e, tt=tt, cchunk=cchunk, bank=bank, k=k: e.transpose(
                    ps[bank][:, k * 128:(k + 1) * 128],
                    C[:, cchunk * 2048 + tt * 128:cchunk * 2048 + (tt + 1) * 128],
                    sb["ident"][:]),
                    waits=w, inc=(k == 3))
            v_tp = ctx.at("pe")
            op("vector", lambda e, half=half, bank=bank, o0=o0: e.tensor_copy(
                A[:, o0 + half * 512:o0 + (half + 1) * 512], ps[bank][:, 0:512]),
                waits=[("pe", v_tp), ("gp", ost_free[buf])])
            bank_free[bank] = ctx.at("dv")
        v_cp = ctx.at("dv")
        sname = "st0" if tt % 2 == 0 else "st1"
        v_st = op("gpsimd", lambda e, tt=tt, o0=o0: e.dma_start(
            out=dr["out"][tt * 128:(tt + 1) * 128, :], in_=A[:, o0:o0 + 1024]),
            waits=[("dv", v_cp)], dma=True, sem=sname)
        ost_free[buf] = op("gpsimd", lambda e: e.engine_nop(),
                           waits=[(sname, v_st)])


def stage_inputs(L0_attn_qkv, L0_attn_wo, L0_ffn_classifier_weight,
                 L0_ffn_e_score_correction_bias, L0_ffn_gate_up_proj,
                 L0_ffn_down_proj):
    f32 = np.float32
    qkvT = np.ascontiguousarray(np.asarray(L0_attn_qkv, dtype=f32).T)
    q5 = qkvT.reshape(8, 128, 3, 8, 128)
    qkv_st = np.ascontiguousarray(q5.transpose(3, 1, 0, 2, 4)).reshape(NHP, 128, 3072)
    woT = np.ascontiguousarray(np.asarray(L0_attn_wo, dtype=f32).T)
    wo_st = np.ascontiguousarray(
        woT.reshape(8, 128, 8, 128).transpose(1, 0, 2, 3)).reshape(128, 8192)
    wcT = np.ascontiguousarray(np.asarray(L0_ffn_classifier_weight, dtype=f32).T)
    wc_st = np.ascontiguousarray(wcT.reshape(8, 128, E_T).transpose(1, 0, 2)).reshape(128, 320)
    bias_st = np.ascontiguousarray(
        np.broadcast_to(np.asarray(L0_ffn_e_score_correction_bias, dtype=f32)[None, :],
                        (128, E_T)))
    gu = np.asarray(L0_ffn_gate_up_proj, dtype=f32)[:E_R]
    guT = gu.transpose(0, 2, 1)
    gu_st = np.ascontiguousarray(
        guT.reshape(E_R, 8, 128, 1024).transpose(0, 2, 1, 3)).reshape(E_R, 128, 8192)
    dn = np.asarray(L0_ffn_down_proj, dtype=f32)
    dnT = dn.transpose(0, 2, 1)
    dn_st = np.ascontiguousarray(
        dnT.reshape(E_R, 4, 128, 1024).transpose(0, 2, 1, 3)).reshape(E_R, 128, 4096)
    mask_st = build_mask_host()
    return dict(qkv_st=qkv_st, wo_st=wo_st, wc_st=wc_st, bias_st=bias_st,
                gu_st=gu_st, dn_st=dn_st, mask_st=mask_st)


def kernel(L0_attn_x, L0_attn_qkv, L0_attn_wo, L0_ffn_classifier_weight,
           L0_ffn_e_score_correction_bias, L0_ffn_gate_up_proj, L0_ffn_down_proj):
    x = np.asarray(L0_attn_x, dtype=np.float32)
    weights = stage_inputs(L0_attn_qkv, L0_attn_wo, L0_ffn_classifier_weight,
                           L0_ffn_e_score_correction_bias, L0_ffn_gate_up_proj,
                           L0_ffn_down_proj)
    nc = build_program()
    in_maps = []
    for c in range(8):
        m = dict(weights)
        m["x"] = np.ascontiguousarray(x[c])
        in_maps.append(m)
    res = run_bass_kernel_spmd(nc, in_maps, core_ids=list(range(8)))
    out = np.stack([res.results[c]["out"] for c in range(8)], axis=0)
    return out.astype(np.float32)
